# version 09
# baseline (speedup 1.0000x reference)
import numpy as np
import ml_dtypes
import concourse.bass as bass
import concourse.mybir as mybir
from concourse.bass_utils import run_bass_kernel_spmd

F32 = mybir.dt.float32
BF16 = mybir.dt.bfloat16
F16 = mybir.dt.float16
U8 = mybir.dt.uint8
AF = mybir.ActivationFunctionType
OP = mybir.AluOpType
AX = mybir.AxisListType

NCORES = 8
D = 2048
DI = 1536
H = 24
DH = 64
L = 8192
TO = 1024
HALO = 131
TH = TO + HALO
TS_ = TO + 128
Q = 128
NCHK = 9
FF = 8192
TT = 256
NTT = TO // TT
MAGIC = 12582912.0
LN2 = 0.6931471805599453
EPS32 = 1.1920928955078125e-07
DEBUG_TAPS = False
KD = D // 128
KI = DI // 128
KF = FF // 128

DT_SIZE = {F32: 4, BF16: 2, F16: 2, U8: 1}


def _ceil32(x):
    return (x + 31) // 32 * 32


class Alloc:
    def __init__(self, nc, base, limit):
        self.nc, self.base, self.limit = nc, base, limit
        self.free = [(0, limit)]
        self.live = {}

    def alloc(self, name, shape, dtype):
        per = _ceil32(int(np.prod(shape[1:])) * DT_SIZE[dtype])
        for i, (off, size) in enumerate(self.free):
            if size >= per:
                if size == per:
                    del self.free[i]
                else:
                    self.free[i] = (off + per, size - per)
                h = self.nc.virtual_sbuf_tensor(name, list(shape), dtype)
                self.nc._tensor(h.name, h.shape, h.dtype, type="SB")
                mloc = self.nc.lookup_mloc(h)
                mloc.allocated = True
                mloc.addr = self.base + off
                self.live[name] = (off, per)
                return h.ap()
        raise MemoryError(f"SBUF alloc fail {name}: need {per}B free={self.free}")

    def freed(self, *names):
        for name in names:
            off, per = self.live.pop(name)
            self.free.append((off, per))
        self.free.sort()
        m = []
        for off, size in self.free:
            if m and m[-1][0] + m[-1][1] == off:
                m[-1] = (m[-1][0], m[-1][1] + size)
            else:
                m.append((off, size))
        self.free = m


class Prog:
    ENGINES = ["sync", "vector", "scalar", "tensor", "gpsimd"]

    def __init__(self, nc, block):
        self.nc = nc
        self.block = block
        self.sem = {}
        self.cnt = {}
        for e in self.ENGINES:
            self.sem[e] = nc.alloc_semaphore(name="s_" + e)
            self.cnt[e] = 0
        self.events = {}

    def dsem(self, name):
        if name not in self.sem:
            self.sem[name] = self.nc.alloc_semaphore(name="s_" + name)
            self.cnt[name] = 0
        return name

    def on(self, engname, fn):
        prog = self

        def wrap(eng):
            fn(_Em(prog, engname, eng))
        getattr(self.block, engname)(wrap)


_CHAINED_OPS = {
    "tensor_scalar", "tensor_tensor", "tensor_copy", "scalar_tensor_tensor",
    "tensor_tensor_scan", "tensor_reduce", "reciprocal", "memset", "memzero",
    "activation", "tensor_add", "tensor_mul", "tensor_sub", "tensor_max",
    "reduce_sum", "reduce_max", "select", "copy", "add", "mul",
}


class _Em:
    def __init__(self, prog, engname, eng):
        self.p = prog
        self.name = engname
        self.eng = eng
        self.chain = engname in ("vector", "scalar", "gpsimd")

    def mark(self, inst, label=None):
        if self.chain:
            if label:
                self.p.events[label] = (self.name, self.p.cnt[self.name])
            return inst
        s = self.p.sem[self.name]
        inst.then_inc(s, 1)
        self.p.cnt[self.name] += 1
        if label:
            self.p.events[label] = (self.name, self.p.cnt[self.name])
        return inst

    def dma(self, inst, label=None, q=None):
        q = self.p.dsem(q)
        inst.then_inc(self.p.sem[q], 16)
        self.p.cnt[q] += 16
        if label:
            self.p.events[label] = (q, self.p.cnt[q])
        return inst

    def wait(self, label):
        key, c = self.p.events[label]
        self.eng.wait_ge(self.p.sem[key], c)

    def wait_cnt(self, key, c):
        self.eng.wait_ge(self.p.sem[key], c)

    def waitif(self, label):
        if label in self.p.events:
            self.wait(label)

    def barrier_wait(self, snapshot):
        for key, c in snapshot.items():
            if c > 0:
                self.eng.wait_ge(self.p.sem[key], c)

    def __getattr__(self, a):
        f = getattr(self.eng, a)
        if self.chain and a in _CHAINED_OPS:
            em = self

            def wrapped(*args, **kwargs):
                s = em.p.sem[em.name]
                if em.p.cnt[em.name] > 0:
                    em.eng.wait_ge(s, em.p.cnt[em.name])
                inst = f(*args, **kwargs)
                inst.then_inc(s, 1)
                em.p.cnt[em.name] += 1
                return inst
            return wrapped
        return f


def nsplit(total, start=0, step=512):
    out = []
    c = start
    while c < total:
        n = min(step, total - c)
        out.append((c, n))
        c += n
    return out


def build_program():
    nc = bass.Bass("TRN2", target_bir_lowering=False)

    din = {}

    def I(name, shape, dtype=F32):
        din[name] = nc.dram_tensor(name, list(shape), dtype, kind="ExternalInput").ap()
        return din[name]

    x_in = I("x_in", (D, TH))
    w_ip = I("w_ip", (24, 128, KD, 128))
    w_xp = I("w_xp", (24, 128, KI, 128))
    w_xpd = I("w_xpd", (128, KI, 24))
    w_op = I("w_op", (16, 128, KI, 128))
    w_gu_nat = I("w_gu_nat", (2 * FF, D))
    w_pr_nat = I("w_pr_nat", (D, FF))
    for nme, shape in [("ipb", (128, 24)), ("xpb", (128, 25)), ("opb", (128, 16)),
                       ("convb", (128, 12)), ("convw", (128, 48)), ("nw", (128, 16)),
                       ("consts2", (24, 8)), ("rg", (1, 1)), ("idf", (128, 128)),
                       ("maskp", (128, 128)), ("selh", (24, 24 * 128)),
                       ("sel64", (24, 24 * 64)), ("selxb", (128, 12 * 24)),
                       ("selpr", (24, 12 * 128)),
                       ("ones1", (1, 128)), ("hmask1", (1, TH))]:
        I(nme, shape)

    y_out = nc.dram_tensor("y_out", [D, TO], F32, kind="ExternalOutput").ap()

    silz_s = nc.dram_tensor("silz_s", [DI, TO], F16).ap()
    x1_s = nc.dram_tensor("x1_s", [D, TO], F32).ap()
    dch_s = nc.dram_tensor("dch_s", [H, NCHK], F32).ap()
    wqgu_s = nc.dram_tensor("wqgu_s", [2 * FF, D], BF16).ap()
    wqpr_s = nc.dram_tensor("wqpr_s", [D, FF], BF16).ap()

    ARENA = 212832
    arena_cm = nc.sbuf_tensor("arena", [128, ARENA], U8, side="left")
    arena_h = arena_cm.__enter__()
    al = Alloc(nc, nc.lookup_mloc(arena_h).addr, ARENA)

    open_psums = {}
    psum_order = []
    psum_uid = [0]

    def psopen(name, shape, dtype=F32):
        psum_uid[0] += 1
        cm = nc.psum_tensor(f"{name}_u{psum_uid[0]}", list(shape), dtype)
        h = cm.__enter__()
        open_psums[name] = cm
        psum_order.append(name)
        return h.ap()

    def psclose(*names):
        names = sorted(names, key=lambda n: psum_order.index(n), reverse=True)
        for n in names:
            assert psum_order[-1] == n, f"psum close order: {n} not on top {psum_order}"
            psum_order.pop()
            open_psums.pop(n).__exit__(None, None, None)

    with nc.Block() as block:
        P = Prog(nc, block)
        EV = P.events

        def on(engname):
            def deco(fn):
                P.on(engname, fn)
            return deco

        def barrier():
            snap = dict(P.cnt)
            for engname in Prog.ENGINES:
                def f(e, _s=snap):
                    e.barrier_wait(_s)
                P.on(engname, f)

        tapn = [0]

        def tap(name, ap, wait=None):
            if not DEBUG_TAPS:
                return
            tapn[0] += 1
            shape = list(ap.shape)
            t = nc.dram_tensor(f"dbg_{name}", shape, ap.dtype).ap()

            snap = {k: P.cnt[k] for k in ("vector", "scalar", "tensor", "gpsimd")}

            def f(e):
                if wait:
                    e.wait(wait)
                e.barrier_wait(snap)
                e.dma(e.dma_start(out=t, in_=ap), f"tap.{tapn[0]}", q="tap")
                e.wait(f"tap.{tapn[0]}")
            P.on("sync", f)

        C = {}
        for nme, shape, dt in [
            ("idf", [128, 128], F32), ("idh", [128, 128], F16), ("idb", [128, 128], BF16),
            ("selxb", [128, 12 * 24], F32), ("ones1", [1, 128], F32),
            ("ipb", [128, 24], F32), ("xpb", [128, 25], F32), ("opb", [128, 16], F32),
            ("convb", [128, 12], F32), ("convw", [128, 48], F32), ("nw", [128, 16], F32),
            ("c2", [24, 8], F32), ("rg", [1, 1], F32), ("rgrep", [128, 1], F32),
            ("hm1", [1, TH], F32), ("ep", [128, 4], F32), ("ep1", [24, 2], F32),
            ("onesc", [128, 1], F32),
        ]:
            C[nme] = al.alloc(nme, shape, dt)
        RW = {}
        for nme in ["mm", "r1", "cs2"]:
            RW[nme] = al.alloc("row_" + nme, [1, TH], F32)
        rA = al.alloc("ringA", [128, TH], F32)
        rB = al.alloc("ringB", [128, TH], F32)
        repA = al.alloc("repA", [128, TH], F32)
        repB = al.alloc("repB", [128, TH], F32)

        @on("sync")
        def _(e):
            for nme, src in [("idf", "idf"), ("selxb", "selxb"), ("ones1", "ones1"),
                             ("ipb", "ipb"), ("xpb", "xpb"), ("opb", "opb"),
                             ("convb", "convb"), ("convw", "convw"), ("nw", "nw"),
                             ("c2", "consts2"), ("rg", "rg"), ("hm1", "hmask1")]:
                e.dma(e.dma_start(out=C[nme], in_=din[src]), "ld." + nme, q="c0")
            P.events["ld.all"] = ("c0", P.cnt["c0"])

        @on("vector")
        def _(e):
            e.wait("ld.all")
            e.wait("ld.all")
            e.memset(C["ep"][:, 0:1], 1e-5)
            e.memset(C["ep"][:, 1:2], EPS32)
            e.memset(C["ep"][:, 2:3], 1.0)
            e.memset(C["ep"][:, 3:4], 0.0)
            e.memset(C["ep1"][:, 0:1], 1.0)
            e.memset(C["onesc"], 1.0)
            e.tensor_copy(C["idh"], C["idf"])
            e.mark(e.tensor_copy(C["idb"], C["idf"]), "p0.casts")

        x0r = [al.alloc(f"x0r{i}", [128, TH], F32) for i in range(2)]
        xnb = al.alloc("xnb", [128, KD * TH], BF16)
        NTH = nsplit(TH)
        pq = [psopen(f"pq{i}", [128, 512]) for i in range(2)]
        prm = [psopen(f"prm{i}", [1, 512]) for i in range(3)]
        prs = [psopen(f"prs{i}", [1, 512]) for i in range(3)]

        @on("tensor")
        def _(e):
            e.wait("p0.casts")
            e.wait("ld.all")
            e.wait("ld.all")
            e.mark(e.matmul(pq[0][:, 0:1], C["ones1"], C["rg"], start=True, stop=True), "A.rg")

        @on("vector")
        def _(e):
            e.wait("A.rg")
            e.mark(e.tensor_copy(C["rgrep"], pq[0][:, 0:1]), "A.rgrep")

        for k in range(KD):
            @on("sync")
            def _(e, k=k):
                if k >= 2:
                    e.wait(f"A.mm.{k-2}")
                e.dma(e.dma_start(out=x0r[k % 2], in_=x_in[128 * k:128 * (k + 1), :]), f"A.x0.{k}", q=f"ax{k % 2}")

            @on("scalar")
            def _(e, k=k):
                e.wait(f"A.x0.{k}")
                if k >= 1:
                    e.wait(f"A.mm.{k-1}")
                e.mark(e.activation(out=rA, in_=x0r[k % 2], func=AF.Square), f"A.sq.{k}")

            @on("tensor")
            def _(e, k=k):
                e.wait(f"A.sq.{k}")
                if k == 0:
                    e.wait("A.rgrep")
                i = None
                for ci, (c0, n) in enumerate(NTH):
                    e.matmul(prm[ci][:, 0:n], C["onesc"], x0r[k % 2][:, c0:c0 + n],
                             start=(k == 0), stop=(k == KD - 1))
                    i = e.matmul(prs[ci][:, 0:n], C["onesc"], rA[:, c0:c0 + n],
                                 start=(k == 0), stop=(k == KD - 1))
                e.mark(i, f"A.mm.{k}")

        @on("vector")
        def _(e):
            e.wait(f"A.mm.{KD-1}")
            for ci, (c0, n) in enumerate(NTH):
                e.tensor_scalar(out=RW["mm"][:, c0:c0 + n], in0=prm[ci][:, 0:n],
                                scalar1=1.0 / D, scalar2=None, op0=OP.mult)
                e.tensor_scalar(out=rA[0:1, c0:c0 + n], in0=prs[ci][:, 0:n],
                                scalar1=1.0 / D, scalar2=None, op0=OP.mult)
            e.mark(e.tensor_copy(rA[0:1, 0:1], rA[0:1, 0:1]), "A.r0")

        @on("scalar")
        def _(e):
            e.wait("A.r0")
            e.mark(e.activation(out=RW["cs2"], in_=RW["mm"], func=AF.Square), "A.r1")

        @on("vector")
        def _(e):
            e.wait("A.r1")
            e.mark(e.tensor_tensor(out=rB[0:1, :], in0=rA[0:1, :], in1=RW["cs2"], op=OP.subtract),
                   "A.r2")

        @on("scalar")
        def _(e):
            e.wait("A.r2")
            e.activation(out=RW["r1"], in_=rB[0:1, :], func=AF.Ln, scale=1.0, bias=C["ep"][0:1, 0:1])
            e.mark(e.activation(out=RW["r1"], in_=RW["r1"], func=AF.Exp, scale=-0.5), "A.r3")

        @on("vector")
        def _(e):
            e.wait("A.r3")
            e.tensor_tensor(out=RW["cs2"], in0=rB[0:1, :], in1=RW["r1"], op=OP.mult)
            e.mark(e.tensor_tensor(out=RW["cs2"], in0=RW["cs2"], in1=RW["r1"], op=OP.mult), "A.r4")

        @on("scalar")
        def _(e):
            e.wait("A.r4")
            e.activation(out=RW["cs2"], in_=RW["cs2"], func=AF.Ln, scale=1.0, bias=C["ep"][0:1, 1:2])
            e.mark(e.activation(out=RW["cs2"], in_=RW["cs2"], func=AF.Exp, scale=-0.5), "A.r5")

        @on("vector")
        def _(e):
            e.wait("A.r5")
            e.mark(e.tensor_tensor(out=RW["cs2"], in0=RW["cs2"], in1=RW["r1"], op=OP.mult), "A.rows")

        for ci, (c0, n) in enumerate(NTH):
            @on("tensor")
            def _(e, ci=ci, c0=c0, n=n):
                e.wait("A.rows")
                if ci >= 1:
                    e.wait(f"A.bcev.{ci-1}")
                e.matmul(pq[0][:, 0:n], C["ones1"], RW["mm"][:, c0:c0 + n], start=True, stop=True)
                e.mark(e.matmul(pq[1][:, 0:n], C["ones1"], RW["cs2"][:, c0:c0 + n],
                                start=True, stop=True), f"A.bc.{ci}")

            @on("vector")
            def _(e, ci=ci, c0=c0, n=n):
                e.wait(f"A.bc.{ci}")
                e.tensor_copy(repA[:, c0:c0 + n], pq[0][:, 0:n])
                e.mark(e.tensor_copy(repB[:, c0:c0 + n], pq[1][:, 0:n]), f"A.bcev.{ci}")

        for k in range(KD):
            @on("sync")
            def _(e, k=k):
                if k == 0:
                    e.wait(f"A.bcev.{len(NTH)-1}")
                if k >= 2:
                    e.wait(f"A.sp.{k-2}")
                e.dma(e.dma_start(out=x0r[k % 2], in_=x_in[128 * k:128 * (k + 1), :]), f"A.x0b.{k}", q=f"ax{k % 2}")

            @on("vector")
            def _(e, k=k):
                e.wait(f"A.x0b.{k}")
                e.tensor_tensor(out=rB, in0=x0r[k % 2], in1=repA, op=OP.subtract)
                e.tensor_tensor(out=rB, in0=rB, in1=repB, op=OP.mult)
                e.mark(e.tensor_scalar(out=xnb[:, k * TH:(k + 1) * TH], in0=rB,
                                       scalar1=C["nw"][:, k:k + 1], scalar2=None,
                                       op0=OP.mult), f"A.sp.{k}")

        tap("xnb", xnb)
        tap("mm", RW["mm"])
        tap("r1", RW["r1"])
        barrier()
        al.freed("x0r0", "x0r1", "row_cs2")
        psclose("prm0", "prm1", "prm2", "prs0", "prs1", "prs2")

        pg = [psopen(f"pg{i}", [128, 512]) for i in range(3)]
        wb16 = [al.alloc(f"wb16{i}", [128, KD * 128], BF16) for i in range(2)]
        xpart = al.alloc("xpart", [128, KI * TH], F32)
        szr = al.alloc("szr", [128, TO], F16)
        NZ = nsplit(TH, HALO)

        def oc_chunks_B(oc):
            return NTH if oc < 12 else NZ

        slotB = [0]
        for oc in range(24):
            @on("gpsimd")
            def _(e, oc=oc):
                if oc >= 2:
                    e.wait(f"B.mm.{oc-2}")
                e.dma(e.dma_start(out=wb16[oc % 2], in_=w_ip[oc]), f"B.wl.{oc}",
                      q=f"bw{oc % 2}")

            for ci, (c0, n) in enumerate(oc_chunks_B(oc)):
                s = slotB[0]

                @on("tensor")
                def _(e, oc=oc, ci=ci, c0=c0, n=n, s=s):
                    par = oc % 2
                    if ci == 0:
                        e.wait(f"B.wl.{oc}")
                    if s >= 3:
                        e.wait(f"B.ev.{s-3}")
                    bank = pg[s % 3]
                    i = None
                    for k in range(KD):
                        i = e.matmul(bank[:, 0:n], wb16[par][:, k * 128:(k + 1) * 128],
                                     xnb[:, k * TH + c0:k * TH + c0 + n],
                                     start=(k == 0), stop=(k == KD - 1))
                    e.mark(i, f"B.psum.{s}")
                    if ci == len(oc_chunks_B(oc)) - 1:
                        EV[f"B.mm.{oc}"] = EV[f"B.psum.{s}"]

                @on("scalar")
                def _(e, oc=oc, ci=ci, c0=c0, n=n, s=s):
                    e.wait(f"B.psum.{s}")
                    bank = pg[s % 3]
                    if oc < 12:
                        e.mark(e.activation(out=xpart[:, oc * TH + c0:oc * TH + c0 + n],
                                            in_=bank[:, 0:n], func=AF.Identity,
                                            bias=C["ipb"][:, oc:oc + 1]), f"B.ev.{s}")
                    else:
                        if ci == 0 and oc >= 13:
                            e.wait(f"B.szdma.{oc-13}")
                        e.mark(e.activation(out=szr[:, c0 - HALO:c0 - HALO + n],
                                            in_=bank[:, 0:n], func=AF.Silu,
                                            bias=C["ipb"][:, oc:oc + 1]), f"B.ev.{s}")
                        if ci == len(oc_chunks_B(oc)) - 1:
                            j = oc - 12
                            e.wait_cnt("scalar", P.cnt["scalar"])
                            e.dma(e.dma_start(out=silz_s[j * 128:(j + 1) * 128, :], in_=szr),
                                  f"B.szdma.{j}", q="bsz")
                slotB[0] += 1

        @on("tensor")
        def _(e):
            e.wait("B.ev.35")
            e.mark(e.matmul(pq[0][:, 0:HALO], C["ones1"], C["hm1"][:, 0:HALO], start=True, stop=True),
                   "B.hmrep")

        @on("vector")
        def _(e):
            e.wait("B.hmrep")
            i = None
            for j in range(KI):
                i = e.tensor_tensor(out=xpart[:, j * TH:j * TH + HALO],
                                    in0=xpart[:, j * TH:j * TH + HALO],
                                    in1=pq[0][:, 0:HALO], op=OP.mult)
            e.mark(i, "B.hm")

        tap("xpart", xpart)
        barrier()
        al.freed("xnb", "wb160", "wb161", "szr")
        psclose("pg0", "pg1", "pg2")

        xsb = al.alloc("xsb", [128, KI * TS_], BF16)
        xsT = al.alloc("xsT", [128, NCHK * DI], F16)
        x16s = al.alloc("x16s", [128, TS_], F16)
        pxb = [psopen(f"pxb{i}", [24, 512]) for i in range(3)]
        psT = psopen("psT", [128, 128], F16)
        NTS = nsplit(TS_)

        for j in range(KI):
            @on("vector")
            def _(e, j=j):
                base = j * TH
                if j >= 1:
                    e.wait(f"C.silu.{j-1}")
                e.tensor_scalar(out=rA[:, 0:TS_], in0=xpart[:, base:base + TS_],
                                scalar1=C["convw"][:, 4 * j:4 * j + 1], scalar2=None, op0=OP.mult)
                i = None
                for kk in range(1, 4):
                    i = e.scalar_tensor_tensor(out=rA[:, 0:TS_],
                                               in0=xpart[:, base + kk:base + kk + TS_],
                                               scalar=C["convw"][:, 4 * j + kk:4 * j + kk + 1],
                                               in1=rA[:, 0:TS_], op0=OP.mult, op1=OP.add)
                e.mark(i, f"C.conv.{j}")

            @on("scalar")
            def _(e, j=j):
                e.wait(f"C.conv.{j}")
                if j >= 1:
                    e.wait(f"C.x16.{j-1}")
                    e.wait(f"C.xb.{j-1}")
                e.mark(e.activation(out=rB[:, 0:TS_], in_=rA[:, 0:TS_], func=AF.Silu,
                                    bias=C["convb"][:, j:j + 1]), f"C.silu.{j}")

            @on("vector")
            def _(e, j=j):
                e.wait(f"C.silu.{j}")
                if j >= 1:
                    e.wait(f"C.tr.{j-1}.{NCHK-1}")
                e.tensor_copy(xsb[:, j * TS_:(j + 1) * TS_], rB[:, 0:TS_])
                e.mark(e.tensor_copy(x16s, rB[:, 0:TS_]), f"C.x16.{j}")

            @on("tensor")
            def _(e, j=j):
                e.wait(f"C.silu.{j}")
                i = None
                for ci, (c0, n) in enumerate(NTS):
                    i = e.matmul(pxb[ci][:, 0:n], C["selxb"][:, j * 24:(j + 1) * 24],
                                 rB[:, c0:c0 + n], start=(j == 0), stop=(j == KI - 1))
                e.mark(i, f"C.xb.{j}")

            for t in range(NCHK):
                @on("tensor")
                def _(e, j=j, t=t):
                    if t == 0:
                        e.wait(f"C.x16.{j}")
                    if not (j == 0 and t == 0):
                        pj, pt = (j, t - 1) if t > 0 else (j - 1, NCHK - 1)
                        e.wait(f"C.trev.{pj}.{pt}")
                    e.mark(e.transpose(psT, x16s[:, t * 128:(t + 1) * 128], C["idh"]), f"C.tr.{j}.{t}")

                @on("vector")
                def _(e, j=j, t=t):
                    e.wait(f"C.tr.{j}.{t}")
                    e.mark(e.tensor_copy(xsT[:, t * DI + j * 128:t * DI + (j + 1) * 128],
                                         psT), f"C.trev.{j}.{t}")

        tap("xsb", xsb)
        tap("xsT", xsT)
        barrier()
        al.freed("xpart", "x16s")
        psclose("psT")

        xbar = al.alloc("xbar", [24, TS_], F32)
        nB2 = al.alloc("nB2", [24, TS_], F32)
        nC2 = al.alloc("nC2", [24, TO], F32)
        dtr = al.alloc("dtr", [24, TS_], F32)

        @on("vector")
        def _(e):
            i = None
            for ci, (c0, n) in enumerate(NTS):
                i = e.tensor_scalar(out=xbar[:, c0:c0 + n], in0=pxb[ci][0:24, 0:n],
                                    scalar1=1.0 / 64, scalar2=None, op0=OP.mult)
            e.mark(i, "C.xbar")

        barrier()

        pg = [psopen(f"pg{i}", [128, 512]) for i in range(3)]
        wb16 = [al.alloc(f"wb16{i}", [128, KI * 128], BF16) for i in range(2)]
        b16 = al.alloc("b16", [128, KI * TS_], F16)
        c16 = al.alloc("c16", [128, KI * TO], F16)
        NBD = NTS
        NCD = [(128, 512), (640, 512)]

        def oc_chunks_D(oc):
            return NBD if (oc < 12 or oc == 24) else NCD

        evring = [rA, rB]
        sqring = [repA, repB]
        slotD = [0]
        for oc in range(25):
            @on("gpsimd")
            def _(e, oc=oc):
                if oc >= 2:
                    e.wait(f"D.mm.{oc-2}")
                if oc < 24:
                    e.dma(e.dma_start(out=wb16[oc % 2], in_=w_xp[oc]), f"D.wl.{oc}",
                          q=f"dw{oc % 2}")
                else:
                    e.dma(e.dma_start(out=wb16[oc % 2][:, 0:KI * 24], in_=w_xpd),
                          f"D.wl.{oc}", q=f"dw{oc % 2}")

            for ci, (c0, n) in enumerate(oc_chunks_D(oc)):
                s = slotD[0]
                M = 128 if oc < 24 else 24

                @on("tensor")
                def _(e, oc=oc, ci=ci, c0=c0, n=n, s=s, M=M):
                    par = oc % 2
                    if ci == 0:
                        e.wait(f"D.wl.{oc}")
                    if s >= 3:
                        e.wait(f"D.cast.{s-3}")
                        e.wait(f"D.nrm.{s-3}")
                    bank = pg[s % 3]
                    i = None
                    for k in range(KI):
                        i = e.matmul(bank[0:M, 0:n], wb16[par][:, k * M:(k + 1) * M],
                                     xsb[:, k * TS_ + c0:k * TS_ + c0 + n],
                                     start=(k == 0), stop=(k == KI - 1))
                    e.mark(i, f"D.psum.{s}")
                    if ci == len(oc_chunks_D(oc)) - 1:
                        EV[f"D.mm.{oc}"] = EV[f"D.psum.{s}"]

                @on("vector")
                def _(e, oc=oc, ci=ci, c0=c0, n=n, s=s, M=M):
                    e.wait(f"D.psum.{s}")
                    if s >= 2:
                        e.wait(f"D.cast.{s-2}")
                        e.wait(f"D.nrm.{s-2}")
                    bank = pg[s % 3]
                    bias = C["xpb"][:, oc:oc + 1] if oc < 24 else C["xpb"][0:24, 24:25]
                    e.scalar_tensor_tensor(out=evring[s % 2][0:M, 0:n], in0=bank[0:M, 0:n],
                                           scalar=bias, in1=sqring[s % 2][0:M, 0:n],
                                           op0=OP.add, op1=OP.bypass)
                    e.mark(e.tensor_tensor(out=sqring[s % 2][0:M, 0:n],
                                           in0=evring[s % 2][0:M, 0:n],
                                           in1=evring[s % 2][0:M, 0:n],
                                           op=OP.mult), f"D.ev.{s}")

                @on("vector")
                def _(e, oc=oc, ci=ci, c0=c0, n=n, s=s):
                    e.wait(f"D.ev.{s}")
                    if oc < 12:
                        i = e.tensor_copy(b16[:, oc * TS_ + c0:oc * TS_ + c0 + n],
                                          evring[s % 2][:, 0:n])
                    elif oc < 24:
                        j = oc - 12
                        i = e.tensor_copy(c16[:, j * TO + c0 - 128:j * TO + c0 - 128 + n],
                                          evring[s % 2][:, 0:n])
                    else:
                        i = e.tensor_copy(dtr[0:24, c0:c0 + n], evring[s % 2][0:24, 0:n])
                    e.mark(i, f"D.cast.{s}")

                @on("tensor")
                def _(e, oc=oc, ci=ci, c0=c0, n=n, s=s):
                    if oc < 12:
                        e.wait(f"D.ev.{s}")
                        e.mark(e.matmul(pxb[ci][:, 0:n], C["selxb"][:, oc * 24:(oc + 1) * 24],
                                        sqring[s % 2][:, 0:n], start=(oc == 0), stop=(oc == 11)),
                               f"D.nrm.{s}")
                    elif oc < 24:
                        e.wait(f"D.ev.{s}")
                        j = oc - 12
                        e.mark(e.matmul(pq[ci][0:24, 0:n], C["selxb"][:, j * 24:(j + 1) * 24],
                                        sqring[s % 2][:, 0:n], start=(j == 0), stop=(j == 11)),
                               f"D.nrm.{s}")
                    else:
                        EV[f"D.nrm.{s}"] = EV[f"D.cast.{s}"]
                slotD[0] += 1

        barrier()

        @on("vector")
        def _(e):
            i = None
            for ci, (c0, n) in enumerate(NTS):
                i = e.tensor_copy(nB2[:, c0:c0 + n], pxb[ci][0:24, 0:n])
            e.tensor_copy(nC2[:, 0:512], pq[0][0:24, 0:512])
            e.mark(e.tensor_copy(nC2[:, 512:1024], pq[1][0:24, 0:512]), "D.fin")

        tap("b16", b16)
        tap("c16raw", c16)
        tap("dtr", dtr)
        tap("nB2", nB2)
        tap("nC2", nC2)
        tap("xbar", xbar)
        barrier()
        al.freed("xsb", "wb160", "wb161")
        psclose("pg0", "pg1", "pg2", "pxb0", "pxb1", "pxb2")

        SC = {}
        SC["Lloc"] = al.alloc("Lloc", [24, TS_], F32)
        SC["dec"] = al.alloc("dec", [24, TS_], F32)
        SC["w2"] = al.alloc("w2", [24, TS_], F32)
        SC["sGT"] = al.alloc("sGT", [128, NCHK * 24], F32)
        SC["nLT"] = al.alloc("nLT", [128, NCHK * 24], F32)
        SC["dchR"] = al.alloc("dchR", [64, H * NCHK], F32)
        SC["SA"] = al.alloc("SA", [64, H * DH], F32)
        SC["SB"] = al.alloc("SB", [64, H * DH], F32)
        C["selh"] = al.alloc("selh", [24, 24 * 128], F32)
        C["sel64"] = al.alloc("sel64", [24, 24 * 64], F32)
        C["selpr"] = al.alloc("selpr", [24, 12 * 128], F32)
        C["maskp"] = al.alloc("maskp", [128, 128], F32)
        rr = [al.alloc(f"rr{i}", [24, TS_], F32) for i in range(3)]
        hc = {n: al.alloc("hc_" + n, [24, 1], F32) for n in ["clA", "ab", "omab", "beta", "emr"]}

        @on("sync")
        def _(e):
            e.dma(e.dma_start(out=C["selh"], in_=din["selh"]), "ld.selh", q="cE")
            e.dma(e.dma_start(out=C["sel64"], in_=din["sel64"]), "ld.sel64", q="cE")
            e.dma(e.dma_start(out=C["maskp"], in_=din["maskp"]), "ld.maskp", q="cE")
            e.dma(e.dma_start(out=C["selpr"], in_=din["selpr"]), "ld.selpr", q="cE")
            P.events["ld.cE"] = ("cE", P.cnt["cE"])

        @on("vector")
        def _(e):
            e.tensor_scalar(out=hc["clA"], in0=C["c2"][0:24, 0:1], scalar1=-2.3, scalar2=-0.01,
                            op0=OP.max, op1=OP.min)
            e.mark(e.tensor_scalar(out=hc["ab"], in0=C["c2"][0:24, 1:2], scalar1=-3.32,
                                   scalar2=-0.015, op0=OP.max, op1=OP.min), "E.v1")

        @on("scalar")
        def _(e):
            e.wait("E.v1")
            e.mark(e.activation(out=hc["ab"], in_=hc["ab"], func=AF.Exp, scale=LN2), "E.a1")

        @on("vector")
        def _(e):
            e.wait("E.a1")
            e.tensor_scalar(out=hc["ab"], in0=hc["ab"], scalar1=-1.0, scalar2=1.0,
                            op0=OP.mult, op1=OP.add)
            e.tensor_scalar(out=hc["omab"], in0=hc["ab"], scalar1=-1.0, scalar2=1.0,
                            op0=OP.mult, op1=OP.add)
            e.mark(e.tensor_scalar(out=hc["beta"], in0=C["c2"][0:24, 2:3], scalar1=-2.0,
                                   scalar2=2.0, op0=OP.max, op1=OP.min), "E.v2")

        @on("scalar")
        def _(e):
            e.wait("E.v2")
            e.mark(e.activation(out=hc["beta"], in_=hc["beta"], func=AF.Exp, scale=LN2), "E.a2")

        @on("vector")
        def _(e):
            e.wait("E.a2")
            e.tensor_scalar(out=hc["emr"], in0=C["c2"][0:24, 3:4], scalar1=1e-6, scalar2=None,
                            op0=OP.add)
            e.mark(e.reciprocal(out=hc["emr"], in_=hc["emr"]), "E.v3")

        @on("scalar")
        def _(e):
            e.wait("E.v3")
            e.activation(out=rr[0], in_=dtr, func=AF.Abs)
            e.activation(out=rr[0], in_=rr[0], func=AF.Exp, scale=-1.0)
            e.mark(e.activation(out=rr[0], in_=rr[0], func=AF.Ln, scale=1.0,
                                bias=C["ep1"][:, 0:1]), "E.a3")

        @on("vector")
        def _(e):
            e.wait("E.a3")
            e.tensor_scalar(out=rr[1], in0=dtr, scalar1=0.0, scalar2=None, op0=OP.max)
            e.tensor_tensor(out=rr[0], in0=rr[0], in1=rr[1], op=OP.add)
            e.mark(e.tensor_scalar(out=rr[0], in0=rr[0], scalar1=hc["clA"], scalar2=None,
                                   op0=OP.mult), "E.v4")

        al.freed("dtr")
        rr.append(al.alloc("rr3", [24, TS_], F32))

        @on("scalar")
        def _(e):
            e.wait("E.v4")
            e.activation(out=rr[1], in_=rr[0], func=AF.Exp)
            e.mark(e.activation(out=rr[2], in_=nB2, func=AF.Sqrt), "E.a4")

        @on("vector")
        def _(e):
            e.wait("E.a4")
            e.tensor_scalar(out=rr[2], in0=rr[2], scalar1=1e-12, scalar2=None, op0=OP.max)
            e.reciprocal(out=rr[2], in_=rr[2])
            e.tensor_scalar(out=rr[1], in0=rr[1], scalar1=-1.0, scalar2=1.0,
                            op0=OP.mult, op1=OP.add)
            e.tensor_tensor(out=rr[1], in0=rr[1], in1=xbar, op=OP.mult)
            e.tensor_tensor(out=rr[1], in0=rr[1], in1=rr[1], op=OP.mult)
            e.tensor_tensor(out=rr[3], in0=nB2, in1=rr[2], op=OP.mult)
            e.tensor_tensor(out=rr[3], in0=rr[3], in1=rr[2], op=OP.mult)
            e.tensor_tensor(out=rr[3], in0=rr[3], in1=rr[1], op=OP.mult)
            e.tensor_scalar(out=rr[3], in0=rr[3], scalar1=hc["emr"], scalar2=1.0 / 64,
                            op0=OP.mult, op1=OP.mult)
            e.mark(e.tensor_scalar(out=rr[3], in0=rr[3], scalar1=hc["beta"], scalar2=None,
                                   op0=OP.mult), "E.v5")
        @on("scalar")
        def _(e):
            e.wait("E.v5")
            e.activation(out=rr[3], in_=rr[3], func=AF.Tanh)
            e.mark(e.activation(out=rr[3], in_=rr[3], func=AF.Relu), "E.a5")

        @on("vector")
        def _(e):
            e.wait("E.a5")
            abb = hc["ab"].unsqueeze(-1).broadcast_to([24, 1, TS_])
            e.scalar_tensor_tensor(out=rr[3].rearrange("p (a b) -> p a b", a=1),
                                   in0=rr[3].rearrange("p (a b) -> p a b", a=1),
                                   scalar=hc["omab"], in1=abb, op0=OP.mult, op1=OP.add)
            e.tensor_scalar(out=rr[3], in0=rr[3], scalar1=0.01, scalar2=0.999,
                            op0=OP.max, op1=OP.min)
            e.mark(e.tensor_scalar(out=rr[1], in0=rr[3], scalar1=-1.0, scalar2=1.0,
                                   op0=OP.mult, op1=OP.add), "E.v6")
        @on("vector")
        def _(e):
            e.mark(e.tensor_tensor(out=rr[2], in0=rr[2], in1=rr[1], op=OP.mult), "E.v6b")

        @on("scalar")
        def _(e):
            e.wait("E.v6b")
            e.mark(e.activation(out=rr[1], in_=rr[3], func=AF.Ln), "E.a6")

        @on("vector")
        def _(e):
            e.wait("E.a6")
            for cc in range(NCHK):
                sl = slice(cc * Q, (cc + 1) * Q)
                e.tensor_tensor_scan(out=SC["Lloc"][:, sl], data0=rr[1][:, sl], data1=rr[1][:, sl],
                                     initial=0.0, op0=OP.add, op1=OP.bypass)
            lv = SC["Lloc"].rearrange("p (c q) -> p c q", q=Q)[:, :, Q - 1:Q]
            e.mark(e.tensor_copy(rr[1][:, 0:NCHK], lv.rearrange("p c q -> p (c q)")), "E.v7")

        @on("scalar")
        def _(e):
            e.wait("E.v7")
            e.mark(e.activation(out=SC["dec"], in_=SC["Lloc"], func=AF.Exp), "E.a7")

        @on("vector")
        def _(e):
            e.wait("E.a7")
            lsb = rr[1][:, 0:NCHK].unsqueeze(-1).broadcast_to([24, NCHK, Q])
            e.mark(e.tensor_tensor(out=SC["w2"].rearrange("p (c q) -> p c q", q=Q),
                                   in0=lsb, in1=SC["Lloc"].rearrange("p (c q) -> p c q", q=Q),
                                   op=OP.subtract), "E.v8")

        @on("scalar")
        def _(e):
            e.wait("E.v8")
            e.mark(e.activation(out=SC["w2"], in_=SC["w2"], func=AF.Exp), "E.a8")
            e.mark(e.activation(out=rr[1][:, 0:NCHK], in_=rr[1][:, 0:NCHK], func=AF.Exp), "E.dch")

        @on("vector")
        def _(e):
            e.wait("E.a8")
            e.mark(e.tensor_tensor(out=SC["w2"], in0=SC["w2"], in1=rr[2], op=OP.mult), "E.v9")

        @on("sync")
        def _(e):
            e.wait("E.dch")
            e.dma(e.dma_start(out=dch_s, in_=rr[1][:, 0:NCHK]), "E.dchw", q="edch")

        @on("tensor")
        def _(e):
            e.wait("E.v9")
            e.wait("ld.cE")
            e.wait("ld.cE")
            e.wait("ld.cE")
            e.mark(e.matmul(pq[0][0:24, 0:Q], C["ones1"][:, 0:24], C["hm1"][:, 3:3 + Q],
                            start=True, stop=True), "E.hmrep")

        @on("vector")
        def _(e):
            e.wait("E.hmrep")
            e.tensor_tensor(out=SC["w2"][:, 0:Q], in0=SC["w2"][:, 0:Q], in1=pq[0][0:24, 0:Q],
                            op=OP.mult)
            e.mark(e.tensor_scalar(out=rr[0], in0=SC["Lloc"], scalar1=-1.0, scalar2=None,
                                   op0=OP.mult), "E.v10")

        for cc in range(NCHK):
            @on("tensor")
            def _(e, cc=cc):
                if cc == 0:
                    e.wait("E.v10")
                else:
                    e.wait(f"E.trev.{cc-1}")
                e.transpose(pq[0][0:128, 0:24], rr[2][:, cc * Q:(cc + 1) * Q], C["idf"][0:24, 0:24])
                e.mark(e.transpose(pq[1][0:128, 0:24], rr[0][:, cc * Q:(cc + 1) * Q],
                                   C["idf"][0:24, 0:24]), f"E.tr.{cc}")

            @on("scalar")
            def _(e, cc=cc):
                e.wait(f"E.tr.{cc}")
                e.activation(out=SC["sGT"][:, cc * 24:(cc + 1) * 24], in_=pq[0][0:128, 0:24],
                             func=AF.Identity)
                e.mark(e.activation(out=SC["nLT"][:, cc * 24:(cc + 1) * 24], in_=pq[1][0:128, 0:24],
                                    func=AF.Identity), f"E.trev.{cc}")

        @on("sync")
        def _(e):
            e.wait("E.dchw")
            e.dma(e.dma_start(out=rr[1][0:1, 0:H * NCHK],
                              in_=dch_s.rearrange("a b -> (a b)").unsqueeze(0)), "E.dchr", q="edch")

        @on("tensor")
        def _(e):
            e.wait("E.dchr")
            e.wait(f"E.trev.{NCHK-1}")
            e.mark(e.matmul(pq[0][0:64, 0:H * NCHK], C["ones1"][:, 0:64], rr[1][0:1, 0:H * NCHK],
                            start=True, stop=True), "E.dchrep")

        @on("vector")
        def _(e):
            e.wait("E.dchrep")
            e.mark(e.tensor_copy(SC["dchR"], pq[0][0:64, 0:H * NCHK]), "E.dchR")

        @on("scalar")
        def _(e):
            e.wait("E.dchrep")
            e.mark(e.activation(out=rr[1][:, 0:TO], in_=nC2, func=AF.Sqrt), "E.cn1")

        @on("vector")
        def _(e):
            e.wait("E.cn1")
            e.tensor_scalar(out=rr[1][:, 0:TO], in0=rr[1][:, 0:TO], scalar1=1e-12, scalar2=None,
                            op0=OP.max)
            e.mark(e.reciprocal(out=rr[1][:, 0:TO], in_=rr[1][:, 0:TO]), "E.cn2")

        for j in range(KI):
            for half in range(2):
                idx = j * 2 + half
                c0, n = half * 512, 512

                @on("tensor")
                def _(e, j=j, half=half, idx=idx, c0=c0, n=n):
                    if idx == 0:
                        e.wait("E.cn2")
                        e.wait("ld.cE")
                        e.wait("E.dchR")
                    if idx >= 2:
                        e.wait(f"E.cnm.{idx-2}")
                    e.mark(e.matmul(pq[idx % 2][:, 0:n], C["selpr"][:, j * 128:(j + 1) * 128],
                                    rr[1][:, c0:c0 + n], start=True, stop=True), f"E.cnr.{idx}")

                @on("vector")
                def _(e, j=j, half=half, idx=idx, c0=c0, n=n):
                    e.wait(f"E.cnr.{idx}")
                    e.mark(e.tensor_tensor(out=c16[:, j * TO + c0:j * TO + c0 + n],
                                           in0=c16[:, j * TO + c0:j * TO + c0 + n],
                                           in1=pq[idx % 2][:, 0:n], op=OP.mult), f"E.cnm.{idx}")

        tap("alpha", rr[3], wait="E.v6")
        tap("Lloc", SC["Lloc"], wait="E.v7")
        tap("dec", SC["dec"], wait="E.a7")
        tap("w2", SC["w2"], wait="E.v10")
        tap("sGT", SC["sGT"], wait=f"E.trev.{NCHK-1}")
        tap("dchR", SC["dchR"], wait="E.dchR")
        tap("c16", c16, wait=f"E.cnm.{KI*2-1}")
        barrier()
        al.freed("xbar", "nB2", "nC2", "rr0", "rr1", "rr2", "rr3")
        for n_ in hc:
            al.freed("hc_" + n_)
        psclose("pq0", "pq1")

        ytm2 = [al.alloc(f"ytm{i}", [128, 4 * DI], F16) for i in range(2)]
        b2buf = [al.alloc(f"b2buf{i}", [64, 128], F16) for i in range(2)]
        b2t = [al.alloc(f"b2t{i}", [128, 64], F16) for i in range(2)]
        ebuf = [al.alloc(f"ebuf{i}", [128, 128], F32) for i in range(2)]
        w0buf = [al.alloc(f"w0buf{i}", [128, 128], F32) for i in range(2)]
        wgbuf = [al.alloc(f"wgbuf{i}", [128, 128], F16) for i in range(2)]
        cbuf = [al.alloc(f"cbuf{i}", [64, 128], F16) for i in range(2)]
        s16b = [al.alloc(f"s16b{i}", [64, DH], F16) for i in range(2)]

        psG = psopen("psG", [128, 128], F32)
        psL = psopen("psL", [128, 128], F32)
        psD = psopen("psD", [64, 128], F32)
        psW = psopen("psW", [64, 128], F32)
        psS = psopen("psS", [64, 64], F32)
        psT2 = psopen("psT2", [128, 64], F16)
        psY = [psopen(f"psY{i}", [128, 64], F32) for i in range(2)]

        @on("vector")
        def _(e):
            e.memset(SC["SA"], 0.0)
            e.mark(e.memset(SC["SB"], 0.0), "S.init")

        STEPS = [(cc, h) for cc in range(NCHK) for h in range(H)]
        lastown = {}
        prev_own_of = {}
        po = None
        for si, (cc, h) in enumerate(STEPS):
            prev_own_of[si] = po
            if cc >= 1:
                po = si

        for si, (cc, h) in enumerate(STEPS):
            j, hb = h // 2, 64 * (h % 2)
            own = cc >= 1
            oc0 = (cc - 1) * Q
            pr_ = si % 2
            Scur = SC["SA"] if (cc % 2 == 0) else SC["SB"]
            Snew = SC["SB"] if (cc % 2 == 0) else SC["SA"]

            @on("tensor")
            def _(e, si=si, cc=cc, h=h, j=j, hb=hb, own=own, oc0=oc0, pr_=pr_):
                if si == 0:
                    e.wait("S.init")
                if si >= 1:
                    e.waitif(f"S.wg.{si-1}")
                    e.wait(f"S.dvA.{si-1}")
                if own:
                    e.matmul(psG, b16[hb:hb + 64, j * TS_ + cc * Q:j * TS_ + (cc + 1) * Q],
                             c16[hb:hb + 64, j * TO + oc0:j * TO + oc0 + Q], start=True, stop=True)
                    e.matmul(psL, C["selh"][:, h * 128:(h + 1) * 128],
                             SC["Lloc"][:, cc * Q:(cc + 1) * Q], start=True, stop=True)
                    e.matmul(psD, C["sel64"][:, h * 64:(h + 1) * 64],
                             SC["dec"][:, cc * Q:(cc + 1) * Q], start=True, stop=True)
                e.mark(e.matmul(psW, C["sel64"][:, h * 64:(h + 1) * 64],
                                SC["w2"][:, cc * Q:(cc + 1) * Q], start=True, stop=True),
                       f"S.peA.{si}")

            @on("vector")
            def _(e, si=si, cc=cc, h=h, j=j, hb=hb, own=own, oc0=oc0, pr_=pr_,
                  Scur=Scur):
                e.wait(f"S.peA.{si}")
                if own:
                    if si >= 2:
                        e.waitif(f"S.wg.{si-2}")
                    e.waitif(f"S.w0.{si-2}")
                    e.tensor_tensor(out=ebuf[pr_], in0=C["maskp"], in1=psL, op=OP.add)
                    e.waitif(f"S.y.{si-2}")
                    e.tensor_copy(s16b[pr_], Scur[:, h * DH:(h + 1) * DH])
                e.waitif(f"S.tr.{si-2}")
                e.mark(e.tensor_tensor(out=b2buf[pr_],
                                       in0=b16[hb:hb + 64, j * TS_ + cc * Q:j * TS_ + (cc + 1) * Q],
                                       in1=psW, op=OP.mult), f"S.dvA.{si}")

            if own:
                @on("scalar")
                def _(e, si=si, cc=cc, h=h, pr_=pr_):
                    e.wait(f"S.dvA.{si}")
                    e.waitif(f"S.wg.{si-2}")
                    e.mark(e.activation(out=w0buf[pr_], in_=ebuf[pr_], func=AF.Exp,
                                        bias=SC["nLT"][:, cc * 24 + h:cc * 24 + h + 1]), f"S.w0.{si}")

                @on("vector")
                def _(e, si=si, cc=cc, h=h, j=j, hb=hb, oc0=oc0, pr_=pr_):
                    e.wait(f"S.w0.{si}")
                    e.waitif(f"S.y.{si-2}")
                    e.scalar_tensor_tensor(out=wgbuf[pr_], in0=psG,
                                           scalar=SC["sGT"][:, cc * 24 + h:cc * 24 + h + 1],
                                           in1=w0buf[pr_], op0=OP.mult, op1=OP.mult)
                    e.mark(e.tensor_tensor(out=cbuf[pr_],
                                           in0=c16[hb:hb + 64, j * TO + oc0:j * TO + oc0 + Q],
                                           in1=psD, op=OP.mult), f"S.wg.{si}")

            @on("tensor")
            def _(e, si=si, cc=cc, h=h, own=own, oc0=oc0, pr_=pr_):
                if own:
                    e.wait(f"S.wg.{si}")
                    e.waitif(f"S.yev.{si-2}")
                    e.matmul(psY[pr_], wgbuf[pr_], xsT[:, cc * DI + h * 64:cc * DI + (h + 1) * 64],
                             start=True, stop=False)
                    e.mark(e.matmul(psY[pr_], cbuf[pr_], s16b[pr_], start=False, stop=True),
                           f"S.y.{si}")
                e.wait(f"S.dvA.{si}")
                e.waitif(f"S.b2t.{si-1}")
                e.mark(e.transpose(psT2, b2buf[pr_], C["idh"][0:64, 0:64]), f"S.tr.{si}")

            @on("vector")
            def _(e, si=si, cc=cc, h=h, own=own, oc0=oc0, pr_=pr_):
                e.wait(f"S.tr.{si}")
                e.waitif(f"S.sc.{si-1}")
                e.mark(e.tensor_copy(b2t[pr_], psT2), f"S.b2t.{si}")
                if own:
                    e.wait(f"S.y.{si}")
                    ch = oc0 // Q
                    yb = ytm2[ch // 4]
                    yo = (ch % 4) * DI + h * 64
                    e.mark(e.tensor_copy(yb[:, yo:yo + 64], psY[pr_]), f"S.yev.{si}")

            @on("tensor")
            def _(e, si=si, cc=cc, h=h, pr_=pr_):
                e.wait(f"S.b2t.{si}")
                e.waitif(f"S.done.{si-1}")
                e.mark(e.matmul(psS, b2t[pr_], xsT[:, cc * DI + h * 64:cc * DI + (h + 1) * 64],
                                start=True, stop=True), f"S.sc.{si}")

            @on("vector")
            def _(e, si=si, cc=cc, h=h, pr_=pr_, Scur=Scur, Snew=Snew):
                e.wait(f"S.sc.{si}")
                e.mark(e.scalar_tensor_tensor(out=Snew[:, h * DH:(h + 1) * DH],
                                              in0=Scur[:, h * DH:(h + 1) * DH],
                                              scalar=SC["dchR"][:, h * NCHK + cc:h * NCHK + cc + 1],
                                              in1=psS, op0=OP.mult, op1=OP.add), f"S.done.{si}")

        tap("ytm0", ytm2[0])
        tap("ytm1", ytm2[1])
        tap("Send", SC["SB"] if NCHK % 2 == 1 else SC["SA"])
        barrier()
        al.freed("b16", "c16", "xsT", "Lloc", "dec", "w2", "sGT", "nLT", "dchR", "SA", "SB",
                 "selh", "sel64", "selpr", "maskp")
        for i in range(2):
            al.freed(f"b2buf{i}", f"b2t{i}", f"ebuf{i}", f"w0buf{i}", f"wgbuf{i}", f"cbuf{i}",
                     f"s16b{i}")
        psclose("psG", "psL", "psD", "psW", "psS", "psT2", "psY0", "psY1")

        rC = al.alloc("ringC", [128, TH], F32)
        ygb = al.alloc("ygb", [128, KI * TO], BF16)
        yfm = al.alloc("yfm", [128, TO], F16)
        szr2 = [al.alloc(f"szr2{i}", [128, TO], F16) for i in range(2)]
        psT3 = psopen("psT3", [128, 128], F16)

        for j in range(KI):
            @on("sync")
            def _(e, j=j):
                if j >= 2:
                    e.wait(f"F.g.{j-2}")
                e.dma(e.dma_start(out=szr2[j % 2], in_=silz_s[j * 128:(j + 1) * 128, :]), f"F.sz.{j}", q=f"fsz{j % 2}")

            for t in range(8):
                @on("tensor")
                def _(e, j=j, t=t):
                    if not (j == 0 and t == 0):
                        pj, pt = (j, t - 1) if t > 0 else (j - 1, 7)
                        e.wait(f"F.trev.{pj}.{pt}")
                    yb = ytm2[t // 4]
                    yo = (t % 4) * DI + j * 128
                    e.mark(e.transpose(psT3, yb[:, yo:yo + 128],
                                       C["idh"]), f"F.tr.{j}.{t}")

                @on("vector")
                def _(e, j=j, t=t):
                    e.wait(f"F.tr.{j}.{t}")
                    if t == 0 and j >= 1:
                        e.wait(f"F.g.{j-1}")
                    e.mark(e.tensor_copy(yfm[:, t * 128:(t + 1) * 128], psT3),
                           f"F.trev.{j}.{t}")

            @on("vector")
            def _(e, j=j):
                e.wait(f"F.trev.{j}.7")
                e.wait(f"F.sz.{j}")
                e.mark(e.tensor_tensor(out=ygb[:, j * TO:(j + 1) * TO], in0=yfm,
                                       in1=szr2[j % 2], op=OP.mult), f"F.g.{j}")

        tap("ygb", ygb)
        barrier()
        al.freed("ytm0", "ytm1", "yfm", "szr20", "szr21")
        psclose("psT3")

        pq = [psopen(f"pq{i}", [128, 512]) for i in range(2)]
        NT2 = nsplit(TO)

        for ci, (c0, n) in enumerate(NT2):
            @on("tensor")
            def _(e, ci=ci, c0=c0, n=n):
                if ci >= 1:
                    e.wait(f"F.bce.{ci-1}")
                e.matmul(pq[0][:, 0:n], C["ones1"], RW["mm"][:, HALO + c0:HALO + c0 + n],
                         start=True, stop=True)
                e.mark(e.matmul(pq[1][:, 0:n], C["ones1"], RW["r1"][:, HALO + c0:HALO + c0 + n],
                                start=True, stop=True), f"F.bc.{ci}")

            @on("vector")
            def _(e, ci=ci, c0=c0, n=n):
                e.wait(f"F.bc.{ci}")
                e.tensor_copy(repA[:, c0:c0 + n], pq[0][:, 0:n])
                e.mark(e.tensor_copy(repB[:, c0:c0 + n], pq[1][:, 0:n]), f"F.bce.{ci}")

        psclose("pq0", "pq1")
        pg = [psopen(f"pg{i}", [128, 512]) for i in range(3)]
        pr = [psopen(f"pr{i}", [1, 512]) for i in range(2)]
        pr2 = [psopen(f"pr2{i}", [1, 512]) for i in range(2)]
        wb16 = [al.alloc(f"wb16{i}", [128, KI * 128], BF16) for i in range(2)]
        x0c = [al.alloc(f"x0c{i}", [128, TO], F32) for i in range(2)]
        x1r = [al.alloc(f"x1r{i}", [128, 512], F32) for i in range(2)]
        evF = [rA, rC]

        slotF = [0]
        for oc in range(16):
            @on("gpsimd")
            def _(e, oc=oc):
                if oc >= 2:
                    e.wait(f"F.mm.{oc-2}")
                e.dma(e.dma_start(out=wb16[oc % 2], in_=w_op[oc]), f"F.wl.{oc}",
                      q=f"fw{oc % 2}")

            @on("sync")
            def _(e, oc=oc):
                if oc >= 2:
                    e.wait(f"F.resoc.{oc-2}")
                e.dma(e.dma_start(out=x0c[oc % 2], in_=x_in[128 * oc:128 * (oc + 1), HALO:]),
                      f"F.x0.{oc}", q=f"fx{oc % 2}")

            for ci, (c0, n) in enumerate(NT2):
                s = slotF[0]

                @on("tensor")
                def _(e, oc=oc, ci=ci, c0=c0, n=n, s=s):
                    par = oc % 2
                    if ci == 0:
                        e.wait(f"F.wl.{oc}")
                        if oc == 0:
                            e.wait(f"F.bce.{len(NT2)-1}")
                    if s >= 3:
                        e.wait(f"F.ev.{s-3}")
                    bank = pg[s % 3]
                    i = None
                    for k in range(KI):
                        i = e.matmul(bank[:, 0:n], wb16[par][:, k * 128:(k + 1) * 128],
                                     ygb[:, k * TO + c0:k * TO + c0 + n],
                                     start=(k == 0), stop=(k == KI - 1))
                    e.mark(i, f"F.psum.{s}")
                    if ci == len(NT2) - 1:
                        EV[f"F.mm.{oc}"] = EV[f"F.psum.{s}"]

                @on("vector")
                def _(e, oc=oc, ci=ci, c0=c0, n=n, s=s):
                    e.wait(f"F.psum.{s}")
                    if s >= 2:
                        e.wait(f"F.st.{s-2}")
                        e.wait(f"F.x1w.{s-2}")
                    e.mark(e.scalar_tensor_tensor(out=evF[s % 2][:, 0:n], in0=pg[s % 3][:, 0:n],
                                                  scalar=C["opb"][:, oc:oc + 1],
                                                  in1=x1r[s % 2][:, 0:n],
                                                  op0=OP.add, op1=OP.bypass), f"F.ev.{s}")

                @on("vector")
                def _(e, oc=oc, ci=ci, c0=c0, n=n, s=s):
                    e.wait(f"F.ev.{s}")
                    e.wait(f"F.x0.{oc}")
                    xs0 = x0c[oc % 2][:, c0:c0 + n]
                    e.tensor_tensor(out=rB[:, 0:n], in0=xs0, in1=repA[:, c0:c0 + n], op=OP.subtract)
                    e.tensor_tensor(out=rB[:, 0:n], in0=rB[:, 0:n], in1=repB[:, c0:c0 + n],
                                    op=OP.mult)
                    e.tensor_tensor(out=rB[:, 0:n], in0=rB[:, 0:n], in1=xs0, op=OP.add)
                    e.mark(e.scalar_tensor_tensor(out=x1r[s % 2][:, 0:n], in0=evF[s % 2][:, 0:n],
                                                  scalar=C["rgrep"], in1=rB[:, 0:n],
                                                  op0=OP.mult, op1=OP.add), f"F.res.{s}")
                    if ci == len(NT2) - 1:
                        EV[f"F.resoc.{oc}"] = EV[f"F.res.{s}"]

                @on("scalar")
                def _(e, oc=oc, ci=ci, c0=c0, n=n, s=s):
                    e.wait(f"F.res.{s}")
                    e.dma(e.dma_start(out=x1_s[oc * 128:(oc + 1) * 128, c0:c0 + n],
                                      in_=x1r[s % 2][:, 0:n]), f"F.x1w.{s}", q=f"fxw{s % 2}")

                @on("vector")
                def _(e, oc=oc, ci=ci, c0=c0, n=n, s=s):
                    e.mark(e.tensor_tensor(out=evF[s % 2][:, 0:n], in0=x1r[s % 2][:, 0:n],
                                           in1=x1r[s % 2][:, 0:n], op=OP.mult), f"F.sq.{s}")

                @on("tensor")
                def _(e, oc=oc, ci=ci, c0=c0, n=n, s=s):
                    e.wait(f"F.sq.{s}")
                    e.matmul(pr[ci][:, 0:n], C["onesc"], x1r[s % 2][:, 0:n],
                             start=(oc == 0), stop=(oc == 15))
                    e.mark(e.matmul(pr2[ci][:, 0:n], C["onesc"], evF[s % 2][:, 0:n],
                                    start=(oc == 0), stop=(oc == 15)), f"F.st.{s}")
                slotF[0] += 1

        barrier()
        tap("rm2pre", rA, wait=None) if False else None
        RW["m2"] = al.alloc("row_m2", [1, TO], F32)
        RW["r2"] = al.alloc("row_r2", [1, TO], F32)

        @on("vector")
        def _(e):
            for ci, (c0, n) in enumerate(NT2):
                e.tensor_scalar(out=RW["m2"][:, c0:c0 + n], in0=pr[ci][:, 0:n],
                                scalar1=1.0 / D, scalar2=None, op0=OP.mult)
                e.tensor_scalar(out=RW["r2"][:, c0:c0 + n], in0=pr2[ci][:, 0:n],
                                scalar1=1.0 / D, scalar2=None, op0=OP.mult)
            e.mark(e.tensor_copy(rA[0:1, 0:1], rA[0:1, 0:1]), "G.r0")

        @on("scalar")
        def _(e):
            e.wait("G.r0")
            e.mark(e.activation(out=rA[0:1, 0:TO], in_=RW["m2"], func=AF.Square), "G.r1")

        @on("vector")
        def _(e):
            e.wait("G.r1")
            e.mark(e.tensor_tensor(out=RW["r2"], in0=RW["r2"], in1=rA[0:1, 0:TO], op=OP.subtract),
                   "G.r2")

        @on("scalar")
        def _(e):
            e.wait("G.r2")
            e.activation(out=RW["r2"], in_=RW["r2"], func=AF.Ln, scale=1.0, bias=C["ep"][0:1, 0:1])
            e.mark(e.activation(out=RW["r2"], in_=RW["r2"], func=AF.Exp, scale=-0.5), "G.r3")

        tap("m2", RW["m2"], wait="G.r3")
        tap("r2", RW["r2"], wait="G.r3")
        barrier()
        al.freed("ygb", "wb160", "wb161", "x0c0", "x0c1",
                 "x1r0", "x1r1", "row_mm", "row_r1")
        psclose("pg0", "pg1", "pg2", "pr0", "pr1", "pr20", "pr21")

        l2b = al.alloc("l2b", [128, KD * TT], BF16)
        hb = al.alloc("hb", [128, KF * TT], BF16)
        wnat = [al.alloc(f"wnat{i}", [128, 4096], BF16) for i in range(2)]
        qr = [al.alloc(f"qr{i}", [128, 4096], BF16) for i in range(2)]
        qdd = al.alloc("qdd", [128, 4096], BF16)
        qsc = al.alloc("qsc", [128, 64], BF16)
        qscf = al.alloc("qscf", [128, 64], F32)
        qsum = al.alloc("qsum", [128, 64], F32)
        lhsT1 = [al.alloc(f"lhsT1{i}", [128, KD * 128], BF16) for i in range(2)]
        _l2 = al.alloc("lhsT2", [128, KF * 128], BF16)
        lhsT2 = [_l2, _l2]
        gx = [al.alloc(f"gx{i}", [128, TT], F32) for i in range(2)]
        rrow = al.alloc("rrow", [1, TT], F32)
        rrep = al.alloc("rrep", [128, TT], F32)

        psF = [psopen(f"psF{i}", [128, TT]) for i in range(4)]
        psH = psopen("psH", [1, TT])
        psQT = [psopen(f"psQT{i}", [128, 512], BF16) for i in range(2)]
        pq = [psopen(f"pq{i}", [128, 512]) for i in range(1)]

        for ci, (c0, n) in enumerate(NT2):
            @on("tensor")
            def _(e, ci=ci, c0=c0, n=n):
                e.wait("G.r3")
                if ci >= 1:
                    e.wait(f"G.bce.{ci-1}")
                e.mark(e.matmul(pq[0][:, 0:n], C["ones1"], RW["m2"][:, c0:c0 + n],
                                start=True, stop=True), f"G.bcm.{ci}")

            @on("vector")
            def _(e, ci=ci, c0=c0, n=n):
                e.wait(f"G.bcm.{ci}")
                e.mark(e.tensor_copy(repA[:, c0:c0 + n], pq[0][:, 0:n]), f"G.bcme.{ci}")

            @on("tensor")
            def _(e, ci=ci, c0=c0, n=n):
                e.wait(f"G.bcme.{ci}")
                e.mark(e.matmul(pq[0][:, 0:n], C["ones1"], RW["r2"][:, c0:c0 + n],
                                start=True, stop=True), f"G.bcr.{ci}")

            @on("vector")
            def _(e, ci=ci, c0=c0, n=n):
                e.wait(f"G.bcr.{ci}")
                e.mark(e.tensor_copy(repB[:, c0:c0 + n], pq[0][:, 0:n]), f"G.bce.{ci}")

        tap("grepA", repA, wait=f"G.bce.1")
        tap("grepB", repB, wait=f"G.bce.1")

        def qunit(u):
            if u < 128:
                kk, gi = u // 2, u % 2
                oc = kk if gi == 0 else KF + kk
                src = w_gu_nat[oc * 128:(oc + 1) * 128, :]
                ncol, ngrp, ntr = D, 32, KD
                dst = lhsT1[u % 2]
                dst_off = 0
                scr = wqgu_s[oc * 128:(oc + 1) * 128, :]
                mmlabel = f"G0.w1mm.{u-2}"
                wslabel = f"G0.ws.{u-2}"
            else:
                v = u - 128
                oc, half = v // 2, v % 2
                src = w_pr_nat[oc * 128:(oc + 1) * 128, half * 4096:(half + 1) * 4096]
                ncol, ngrp, ntr = 4096, 64, 32
                dst = lhsT2[oc % 2]
                dst_off = half * 32
                scr = wqpr_s[oc * 128:(oc + 1) * 128, half * 4096:(half + 1) * 4096]
                mmlabel = f"G0.w2mm.{oc-1}"
                wslabel = f"G0.ws.{u-2}"
            par = u % 2

            @on("gpsimd")
            def _(e, u=u, src=src, ncol=ncol, par=par):
                if u >= 2:
                    e.wait(f"G0.wq.{u-2}")
                e.dma(e.dma_start(out=wnat[par][:, 0:ncol], in_=src), f"G0.wn.{u}", q=f"gn{u % 2}")

            @on("vector")
            def _(e, u=u, ncol=ncol, ngrp=ngrp, ntr=ntr, par=par):
                e.wait(f"G0.wn.{u}")
                if u >= 2:
                    e.waitif(f"G0.qtrlast.{u-2}")
                e.scalar_tensor_tensor(out=qr[par][:, 0:ncol], in0=wnat[par][:, 0:ncol],
                                       scalar=-1.0, in1=wnat[par][:, 0:ncol],
                                       op0=OP.mult, op1=OP.max)
                e.tensor_reduce(out=qsum[:, 0:ngrp],
                                in_=qr[par][:, 0:ncol].rearrange("p (g v) -> p g v", v=64),
                                axis=AX.X, op=OP.add)
                e.tensor_scalar(out=qsc[:, 0:ngrp], in0=qsum[:, 0:ngrp], scalar1=1.0 / 64,
                                scalar2=None, op0=OP.mult)
                e.tensor_scalar(out=qsc[:, 0:ngrp], in0=qsc[:, 0:ngrp], scalar1=1e-8,
                                scalar2=None, op0=OP.max)
                e.mark(e.reciprocal(out=qscf[:, 0:ngrp], in_=qsc[:, 0:ngrp]), f"G0.sc.{u}")
                rsb = qscf[:, 0:ngrp].unsqueeze(-1).broadcast_to([128, ngrp, 64])
                scb = qsc[:, 0:ngrp].unsqueeze(-1).broadcast_to([128, ngrp, 64])
                qv = qr[par][:, 0:ncol].rearrange("p (g v) -> p g v", v=64)
                wv = wnat[par][:, 0:ncol].rearrange("p (g v) -> p g v", v=64)
                e.tensor_tensor(out=qv, in0=wv, in1=rsb, op=OP.mult)
                e.tensor_scalar(out=qr[par][:, 0:ncol], in0=qr[par][:, 0:ncol],
                                scalar1=-1.0, scalar2=1.0, op0=OP.max, op1=OP.min)
                e.tensor_scalar(out=qr[par][:, 0:ncol], in0=qr[par][:, 0:ncol],
                                scalar1=MAGIC, scalar2=-MAGIC, op0=OP.add, op1=OP.add)
                e.tensor_tensor(out=qv, in0=qv, in1=scb, op=OP.mult)
                e.tensor_tensor(out=qdd[:, 0:ncol], in0=qr[par][:, 0:ncol],
                                in1=wnat[par][:, 0:ncol], op=OP.subtract)
                e.mark(e.tensor_tensor(out=qr[par][:, 0:ncol], in0=qdd[:, 0:ncol],
                                       in1=wnat[par][:, 0:ncol], op=OP.add), f"G0.wq.{u}")

            for k in range(ntr):
                g, s4 = k // 4, k % 4
                @on("tensor")
                def _(e, u=u, k=k, g=g, s4=s4, par=par, ntr=ntr):
                    if k == 0:
                        e.wait(f"G0.wq.{u}")
                    if g < 2 and s4 == 0:
                        e.waitif(f"G0.qteL{g}.{u-1}")
                    if g >= 2 and s4 == 0:
                        e.wait(f"G0.qte.{u}.{g-2}")
                    e.mark(e.transpose(psQT[g % 2][:, s4 * 128:(s4 + 1) * 128],
                                       qr[par][:, k * 128:(k + 1) * 128], C["idb"]),
                           f"G0.qtr.{u}.{k}")

                if s4 == 3:
                    @on("vector")
                    def _(e, u=u, k=k, g=g, par=par, dst=dst, dst_off=dst_off,
                          mmlabel=mmlabel, wslabel=wslabel, ntr=ntr):
                        e.wait(f"G0.qtr.{u}.{k}")
                        if g == 0:
                            e.waitif(mmlabel)
                            e.waitif(wslabel)
                        e.mark(e.tensor_copy(dst[:, (dst_off + g * 4) * 128:(dst_off + g * 4 + 4) * 128],
                                             psQT[g % 2]), f"G0.qte.{u}.{g}")

            ng = ntr // 4
            EV[f"G0.qtrlast.{u}"] = EV[f"G0.qtr.{u}.{ntr-1}"]
            EV[f"G0.qteL0.{u}"] = EV[f"G0.qte.{u}.{ng-2}"]
            EV[f"G0.qteL1.{u}"] = EV[f"G0.qte.{u}.{ng-1}"]

            @on("scalar")
            def _(e, u=u, scr=scr, dst=dst, dst_off=dst_off, ntr=ntr):
                e.wait(f"G0.qte.{u}.{ntr // 4 - 1}")
                e.wait_cnt("scalar", P.cnt["scalar"])
                e.dma(e.dma_start(out=scr, in_=dst[:, dst_off * 128:(dst_off + ntr) * 128]),
                      f"G0.ws.{u}", q=("gwsa" + str(u % 2)) if u < 128 else ("gwsb" + str(u % 2)))

        for tt in range(NTT):
            for k in range(KD):
                @on("sync")
                def _(e, tt=tt, k=k):
                    if tt == 0 and k == 0:
                        e.wait(f"G.bce.{len(NT2)-1}")
                    if tt > 0 and k == 0:
                        e.wait(f"G{tt-1}.lastmm")
                    if tt > 0 and k < 2:
                        e.wait(f"G{tt-1}.out.{14 + k}")
                    if k >= 2:
                        e.wait(f"G{tt}.l2.{k-2}")
                    e.dma(e.dma_start(out=gx[k % 2], in_=x1_s[128 * k:128 * (k + 1),
                                                           tt * TT:(tt + 1) * TT]), f"G{tt}.x1.{k}", q=f"ggx{k % 2}")

                @on("vector")
                def _(e, tt=tt, k=k):
                    e.wait(f"G{tt}.x1.{k}")
                    if tt > 0 and k == 0:
                        e.wait(f"G{tt-1}.lastmm")
                    e.tensor_tensor(out=rA[:, 0:TT], in0=gx[k % 2],
                                    in1=repA[:, tt * TT:(tt + 1) * TT], op=OP.subtract)
                    e.mark(e.tensor_tensor(out=l2b[:, k * TT:(k + 1) * TT], in0=rA[:, 0:TT],
                                           in1=repB[:, tt * TT:(tt + 1) * TT], op=OP.mult),
                           f"G{tt}.l2.{k}")

            for kk in range(KF):
                for gi in range(2):
                    u = kk * 2 + gi
                    if tt == 0:
                        qunit(u)
                    else:
                        @on("sync")
                        def _(e, tt=tt, u=u, kk=kk, gi=gi):
                            oc = kk if gi == 0 else KF + kk
                            if u >= 2:
                                e.wait(f"G{tt}.w1mm.{u-2}")
                            e.dma(e.dma_start(out=lhsT1[u % 2],
                                              in_=wqgu_s[oc * 128:(oc + 1) * 128, :]),
                                  f"G{tt}.w1l.{u}", q=f"gw1{u % 2}")

                    @on("tensor")
                    def _(e, tt=tt, u=u, kk=kk, gi=gi):
                        par = u % 2
                        if tt == 0:
                            e.wait(f"G0.qte.{u}.{KD // 4 - 1}")
                        else:
                            e.wait(f"G{tt}.w1l.{u}")
                        if kk == 0 and gi == 0:
                            e.wait(f"G{tt}.l2.{KD-1}")
                        if kk >= 2:
                            e.wait(f"G{tt}.h.{kk-2}")
                        pbank = psF[kk % 2] if gi == 0 else psF[2 + kk % 2]
                        i = None
                        for k in range(KD):
                            i = e.matmul(pbank, lhsT1[par][:, k * 128:(k + 1) * 128],
                                         l2b[:, k * TT:(k + 1) * TT],
                                         start=(k == 0), stop=(k == KD - 1))
                        e.mark(i, f"G{tt}.w1mm.{u}")

                @on("scalar")
                def _(e, tt=tt, kk=kk):
                    e.wait(f"G{tt}.w1mm.{kk*2}")
                    e.wait(f"G{tt}.w1mm.{kk*2+1}")
                    if kk >= 1:
                        e.wait(f"G{tt}.st.{kk-1}")
                    e.mark(e.activation(out=rA[:, 0:TT], in_=psF[kk % 2], func=AF.Silu),
                           f"G{tt}.sg.{kk}")

                @on("vector")
                def _(e, tt=tt, kk=kk):
                    e.wait(f"G{tt}.sg.{kk}")
                    if kk == 0 and tt > 0:
                        e.wait(f"G{tt-1}.y.14")
                    if kk >= 1:
                        e.wait(f"G{tt}.st.{kk-1}")
                    e.tensor_tensor(out=rB[:, 0:TT], in0=rA[:, 0:TT], in1=psF[2 + kk % 2],
                                    op=OP.mult)
                    e.mark(e.tensor_copy(hb[:, kk * TT:(kk + 1) * TT], rB[:, 0:TT]),
                           f"G{tt}.h.{kk}")

                @on("vector")
                def _(e, tt=tt, kk=kk):
                    e.wait(f"G{tt}.h.{kk}")
                    e.mark(e.tensor_tensor(out=rA[:, 0:TT], in0=rB[:, 0:TT],
                                           in1=rB[:, 0:TT], op=OP.mult),
                           f"G{tt}.hsq.{kk}")

                @on("tensor")
                def _(e, tt=tt, kk=kk):
                    e.wait(f"G{tt}.hsq.{kk}")
                    e.mark(e.matmul(psH, C["onesc"], rA[:, 0:TT], start=(kk == 0),
                                    stop=(kk == KF - 1)), f"G{tt}.st.{kk}")

            @on("scalar")
            def _(e, tt=tt):
                e.wait(f"G{tt}.st.{KF-1}")
                e.activation(out=rrow, in_=psH, func=AF.Identity, scale=1.0 / FF)
                e.activation(out=rrow, in_=rrow, func=AF.Ln, scale=1.0, bias=C["ep"][0:1, 1:2])
                e.mark(e.activation(out=rrow, in_=rrow, func=AF.Exp, scale=-0.5), f"G{tt}.rr")

            @on("tensor")
            def _(e, tt=tt):
                e.wait(f"G{tt}.rr")
                if tt >= 1:
                    e.wait(f"G{tt-1}.rrepev")
                e.mark(e.matmul(pq[0][:, 0:TT], C["ones1"], rrow, start=True, stop=True),
                       f"G{tt}.rrb")

            @on("vector")
            def _(e, tt=tt):
                e.wait(f"G{tt}.rrb")
                e.mark(e.tensor_copy(rrep, pq[0][:, 0:TT]), f"G{tt}.rrepev")

            if tt == 0:
                tap("rrow", rrow, wait="G0.rr")
                tap("rrep", rrep, wait="G0.rrepev")
                tap("hb", hb, wait=f"G0.h.{KF-1}")
                tap("l2b", l2b, wait=f"G0.l2.{KD-1}")

            for oc in range(16):
                if tt == 0:
                    qunit(128 + oc * 2 + 0)
                    qunit(128 + oc * 2 + 1)
                else:
                    @on("sync")
                    def _(e, tt=tt, oc=oc):
                        if tt == 1 and oc == 0:
                            e.wait("G0.ws.158")
                            e.wait("G0.ws.159")
                        if oc >= 1:
                            e.wait(f"G{tt}.o.{oc-1}")
                        e.dma(e.dma_start(out=lhsT2[oc % 2],
                                          in_=wqpr_s[oc * 128:(oc + 1) * 128, :]), f"G{tt}.w2l.{oc}", q="gw2")

                @on("sync")
                def _(e, tt=tt, oc=oc):
                    if oc >= 2:
                        e.wait(f"G{tt}.out.{oc-2}")
                    else:
                        e.wait(f"G{tt}.l2.{KD-2+oc}")
                    e.dma(e.dma_start(out=gx[oc % 2],
                                      in_=x1_s[oc * 128:(oc + 1) * 128, tt * TT:(tt + 1) * TT]),
                          f"G{tt}.x1b.{oc}", q=f"ggx{oc % 2}")

                @on("tensor")
                def _(e, tt=tt, oc=oc):
                    par = oc % 2
                    if tt == 0:
                        e.wait(f"G0.qte.{128+oc*2+1}.{7}")
                    else:
                        e.wait(f"G{tt}.w2l.{oc}")
                    if oc == 0:
                        e.wait(f"G{tt}.h.{KF-1}")
                    if oc >= 1:
                        e.wait(f"G{tt}.out.{oc-1}")
                    i = None
                    for k in range(KF):
                        i = e.matmul(psF[0], lhsT2[par][:, k * 128:(k + 1) * 128],
                                     hb[:, k * TT:(k + 1) * TT],
                                     start=(k == 0), stop=(k == KF - 1))
                    e.mark(i, f"G{tt}.o.{oc}")
                    if tt == 0:
                        EV[f"G0.w2mm.{oc}"] = EV[f"G{tt}.o.{oc}"]

                @on("vector")
                def _(e, tt=tt, oc=oc):
                    e.wait(f"G{tt}.o.{oc}")
                    e.wait(f"G{tt}.x1b.{oc}")
                    e.wait(f"G{tt}.rrepev")
                    if oc >= 2:
                        e.wait(f"G{tt}.y.{oc-2}")
                    elif oc == 1 and tt > 0:
                        e.wait(f"G{tt-1}.y.15")
                    ring = rB if oc % 2 == 0 else rC
                    e.tensor_tensor(out=ring[:, 0:TT], in0=psF[0], in1=rrep, op=OP.mult)
                    e.mark(e.tensor_tensor(out=ring[:, 0:TT], in0=ring[:, 0:TT],
                                           in1=gx[oc % 2], op=OP.add), f"G{tt}.out.{oc}")

                @on("scalar")
                def _(e, tt=tt, oc=oc):
                    e.wait(f"G{tt}.out.{oc}")
                    ring = rB if oc % 2 == 0 else rC
                    e.dma(e.dma_start(out=y_out[oc * 128:(oc + 1) * 128, tt * TT:(tt + 1) * TT],
                                      in_=ring[:, 0:TT]), f"G{tt}.y.{oc}", q=f"gy{oc % 2}")

            EV[f"G{tt}.lastmm"] = EV[f"G{tt}.o.15"]

        @on("sync")
        def _(e):
            e.wait(f"G{NTT-1}.y.14")
            e.wait(f"G{NTT-1}.y.15")
            e.wait_cnt("gy0", P.cnt["gy0"])
            e.wait_cnt("gy1", P.cnt["gy1"])

    for n in reversed(psum_order):
        open_psums.pop(n).__exit__(None, None, None)
    arena_cm.__exit__(None, None, None)
    return nc



def _prep(inputs):
    f32 = np.float32
    x = np.asarray(inputs["L0_attn_hidden_states"], f32)[0]
    xT = np.ascontiguousarray(x.T)
    xTp = np.zeros((D, L + HALO), f32)
    xTp[:, HALO:] = xT

    ipw = np.asarray(inputs["L0_attn_in_proj_w"], f32)
    w_ip = np.ascontiguousarray(
        ipw.T.reshape(KD, 128, 24, 128).transpose(2, 1, 0, 3))

    xpw = np.asarray(inputs["L0_attn_x_proj_w"], f32)
    perm = []
    for h in range(H):
        perm.extend(range(129 * h + 1, 129 * h + 65))
    for h in range(H):
        perm.extend(range(129 * h + 65, 129 * h + 129))
    dt_rows = [129 * h for h in range(H)]
    xpw_bc = xpw[perm]
    w_xp = np.ascontiguousarray(
        xpw_bc.T.reshape(KI, 128, 24, 128).transpose(2, 1, 0, 3))
    w_xpd = np.ascontiguousarray(
        xpw[dt_rows].T.reshape(KI, 128, H).transpose(1, 0, 2))

    opw = np.asarray(inputs["L0_attn_out_proj_w"], f32)
    w_op = np.ascontiguousarray(
        opw.T.reshape(KI, 128, 16, 128).transpose(2, 1, 0, 3))

    def colblock(v, nb):
        out = np.zeros((128, nb), f32)
        for o in range(nb):
            out[:, o] = v[o * 128:(o + 1) * 128]
        return out

    ipb = colblock(np.asarray(inputs["L0_attn_in_proj_b"], f32), 24)
    xpb_v = np.asarray(inputs["L0_attn_x_proj_b"], f32)
    xpb = np.zeros((128, 25), f32)
    xpb_bc = xpb_v[perm]
    for o in range(24):
        xpb[:, o] = xpb_bc[o * 128:(o + 1) * 128]
    xpb[0:24, 24] = xpb_v[dt_rows]
    opb = colblock(np.asarray(inputs["L0_attn_out_proj_b"], f32), 16)

    cw = np.asarray(inputs["L0_attn_conv_w"], f32)[:, 0, :]
    convw = np.zeros((128, 48), f32)
    for j in range(KI):
        convw[:, 4 * j:4 * (j + 1)] = cw[j * 128:(j + 1) * 128]
    convb = colblock(np.asarray(inputs["L0_attn_conv_b"], f32), 12)
    nw = np.zeros((128, KD), f32)
    nwv = np.asarray(inputs["L0_attn_norm_w"], f32)
    for k in range(KD):
        nw[:, k] = nwv[k * 128:(k + 1) * 128]

    c2 = np.zeros((H, 8), f32)
    c2[:, 0] = np.asarray(inputs["L0_attn_A_log"], f32)
    c2[:, 1] = np.asarray(inputs["L0_attn_log2_alpha_base"], f32)
    c2[:, 2] = np.asarray(inputs["L0_attn_log2_beta"], f32)
    c2[:, 3] = np.asarray(inputs["L0_attn_surprise_ema"], f32)
    rg = np.asarray(inputs["L0_attn_res_gate"], f32).reshape(1, 1)

    idf = np.eye(128, dtype=f32)
    maskp = np.where(np.arange(128)[:, None] <= np.arange(128)[None, :], 0.0, -30000.0).astype(f32)
    selh = np.zeros((H, H * 128), f32)
    for h in range(H):
        selh[h, h * 128:(h + 1) * 128] = 1.0
    sel64 = np.zeros((H, H * 64), f32)
    for h in range(H):
        sel64[h, h * 64:(h + 1) * 64] = 1.0
    selpr = np.zeros((H, 12 * 128), f32)
    for j in range(KI):
        selpr[2 * j, j * 128:j * 128 + 64] = 1.0
        selpr[2 * j + 1, j * 128 + 64:(j + 1) * 128] = 1.0
    selxb = np.zeros((128, 12 * H), f32)
    for j in range(KI):
        selxb[0:64, j * H + 2 * j] = 1.0
        selxb[64:128, j * H + 2 * j + 1] = 1.0
    ones1 = np.ones((1, 128), f32)

    wgu = np.asarray(inputs["L0_ffn_w_gate_up"], f32)
    wpr = np.asarray(inputs["L0_ffn_w_proj"], f32)

    shared = dict(w_ip=w_ip, w_xp=w_xp, w_xpd=w_xpd, w_op=w_op,
                  w_gu_nat=np.ascontiguousarray(wgu), w_pr_nat=np.ascontiguousarray(wpr),
                  ipb=ipb, xpb=xpb, opb=opb, convb=convb, convw=convw, nw=nw,
                  consts2=c2, rg=rg, idf=idf, maskp=maskp, selh=selh, sel64=sel64,
                  selxb=selxb, selpr=selpr, ones1=ones1)

    in_maps = []
    for core in range(NCORES):
        m = dict(shared)
        m["x_in"] = np.ascontiguousarray(xTp[:, TO * core:TO * core + TH])
        hm = np.ones((1, TH), f32)
        if core == 0:
            hm[0, :HALO] = 0.0
        m["hmask1"] = hm
        in_maps.append(m)
    return in_maps


def kernel(**inputs):
    in_maps = _prep(inputs)
    nc = build_program()
    res = run_bass_kernel_spmd(nc, in_maps, core_ids=list(range(NCORES)))
    out = np.concatenate([res.results[c]["y_out"] for c in range(NCORES)], axis=1)
    return np.ascontiguousarray(out.T)[None]


# version 10 (final)
# speedup vs baseline: 1.1969x; 1.1969x over previous
import numpy as np
import ml_dtypes
import concourse.bass as bass
import concourse.mybir as mybir
from concourse.bass_utils import run_bass_kernel_spmd

F32 = mybir.dt.float32
BF16 = mybir.dt.bfloat16
F16 = mybir.dt.float16
U8 = mybir.dt.uint8
AF = mybir.ActivationFunctionType
OP = mybir.AluOpType
AX = mybir.AxisListType

NCORES = 8
D = 2048
DI = 1536
H = 24
DH = 64
L = 8192
TO = 1024
HALO = 131
TH = TO + HALO
TS_ = TO + 128
Q = 128
NCHK = 9
FF = 8192
TT = 256
NTT = TO // TT
MAGIC = 12582912.0
LN2 = 0.6931471805599453
EPS32 = 1.1920928955078125e-07
DEBUG_TAPS = False
KD = D // 128
KI = DI // 128
KF = FF // 128

DT_SIZE = {F32: 4, BF16: 2, F16: 2, U8: 1}


def _ceil32(x):
    return (x + 31) // 32 * 32


class Alloc:
    def __init__(self, nc, base, limit):
        self.nc, self.base, self.limit = nc, base, limit
        self.free = [(0, limit)]
        self.live = {}

    def alloc(self, name, shape, dtype):
        per = _ceil32(int(np.prod(shape[1:])) * DT_SIZE[dtype])
        for i, (off, size) in enumerate(self.free):
            if size >= per:
                if size == per:
                    del self.free[i]
                else:
                    self.free[i] = (off + per, size - per)
                h = self.nc.virtual_sbuf_tensor(name, list(shape), dtype)
                self.nc._tensor(h.name, h.shape, h.dtype, type="SB")
                mloc = self.nc.lookup_mloc(h)
                mloc.allocated = True
                mloc.addr = self.base + off
                self.live[name] = (off, per)
                return h.ap()
        raise MemoryError(f"SBUF alloc fail {name}: need {per}B free={self.free}")

    def freed(self, *names):
        for name in names:
            off, per = self.live.pop(name)
            self.free.append((off, per))
        self.free.sort()
        m = []
        for off, size in self.free:
            if m and m[-1][0] + m[-1][1] == off:
                m[-1] = (m[-1][0], m[-1][1] + size)
            else:
                m.append((off, size))
        self.free = m


class Prog:
    ENGINES = ["sync", "vector", "scalar", "tensor", "gpsimd"]

    def __init__(self, nc, block):
        self.nc = nc
        self.block = block
        self.sem = {}
        self.cnt = {}
        for e in self.ENGINES:
            self.sem[e] = nc.alloc_semaphore(name="s_" + e)
            self.cnt[e] = 0
        self.events = {}

    def dsem(self, name):
        if name not in self.sem:
            self.sem[name] = self.nc.alloc_semaphore(name="s_" + name)
            self.cnt[name] = 0
        return name

    def on(self, engname, fn):
        prog = self

        def wrap(eng):
            fn(_Em(prog, engname, eng))
        getattr(self.block, engname)(wrap)


_CHAINED_OPS = {
    "tensor_scalar", "tensor_tensor", "tensor_copy", "scalar_tensor_tensor",
    "tensor_tensor_scan", "tensor_reduce", "reciprocal", "memset", "memzero",
    "activation", "tensor_add", "tensor_mul", "tensor_sub", "tensor_max",
    "reduce_sum", "reduce_max", "select", "copy", "add", "mul",
}


class _Em:
    def __init__(self, prog, engname, eng):
        self.p = prog
        self.name = engname
        self.eng = eng
        self.chain = engname in ("vector", "scalar", "gpsimd")

    def mark(self, inst, label=None):
        if self.chain:
            if label:
                self.p.events[label] = (self.name, self.p.cnt[self.name])
            return inst
        s = self.p.sem[self.name]
        inst.then_inc(s, 1)
        self.p.cnt[self.name] += 1
        if label:
            self.p.events[label] = (self.name, self.p.cnt[self.name])
        return inst

    def dma(self, inst, label=None, q=None):
        q = self.p.dsem(q)
        inst.then_inc(self.p.sem[q], 16)
        self.p.cnt[q] += 16
        if label:
            self.p.events[label] = (q, self.p.cnt[q])
        return inst

    def wait(self, label):
        key, c = self.p.events[label]
        self.eng.wait_ge(self.p.sem[key], c)

    def wait_cnt(self, key, c):
        self.eng.wait_ge(self.p.sem[key], c)

    def waitif(self, label):
        if label in self.p.events:
            self.wait(label)

    def barrier_wait(self, snapshot):
        for key, c in snapshot.items():
            if c > 0:
                self.eng.wait_ge(self.p.sem[key], c)

    def __getattr__(self, a):
        f = getattr(self.eng, a)
        if self.chain and a in _CHAINED_OPS:
            em = self

            def wrapped(*args, **kwargs):
                s = em.p.sem[em.name]
                if em.p.cnt[em.name] > 0:
                    em.eng.wait_ge(s, em.p.cnt[em.name])
                inst = f(*args, **kwargs)
                inst.then_inc(s, 1)
                em.p.cnt[em.name] += 1
                return inst
            return wrapped
        return f


def nsplit(total, start=0, step=512):
    out = []
    c = start
    while c < total:
        n = min(step, total - c)
        out.append((c, n))
        c += n
    return out


def build_program():
    nc = bass.Bass("TRN2", target_bir_lowering=False)

    din = {}

    def I(name, shape, dtype=F32):
        din[name] = nc.dram_tensor(name, list(shape), dtype, kind="ExternalInput").ap()
        return din[name]

    x_in = I("x_in", (D, TH))
    w_ip = I("w_ip", (24, 128, KD, 128))
    w_xp = I("w_xp", (24, 128, KI, 128))
    w_xpd = I("w_xpd", (128, KI, 24))
    w_op = I("w_op", (16, 128, KI, 128))
    w_gu_nat = I("w_gu_nat", (2 * FF, D))
    w_pr_nat = I("w_pr_nat", (D, FF))
    for nme, shape in [("ipb", (128, 24)), ("xpb", (128, 25)), ("opb", (128, 16)),
                       ("convb", (128, 12)), ("convw", (128, 48)), ("nw", (128, 16)),
                       ("consts2", (24, 8)), ("rg", (1, 1)), ("idf", (128, 128)),
                       ("maskp", (128, 128)), ("selh", (24, 24 * 128)),
                       ("sel64", (24, 24 * 64)), ("selxb", (128, 12 * 24)),
                       ("selpr", (24, 12 * 128)),
                       ("ones1", (1, 128)), ("hmask1", (1, TH))]:
        I(nme, shape)

    y_out = nc.dram_tensor("y_out", [D, TO], F32, kind="ExternalOutput").ap()

    silz_s = nc.dram_tensor("silz_s", [DI, TO], F16).ap()
    x1_s = nc.dram_tensor("x1_s", [D, TO], F32).ap()
    dch_s = nc.dram_tensor("dch_s", [H, NCHK], F32).ap()
    wqgu_s = nc.dram_tensor("wqgu_s", [2 * FF, D], BF16).ap()
    wqpr_s = nc.dram_tensor("wqpr_s", [D, FF], BF16).ap()

    ARENA = 212832
    arena_cm = nc.sbuf_tensor("arena", [128, ARENA], U8, side="left")
    arena_h = arena_cm.__enter__()
    al = Alloc(nc, nc.lookup_mloc(arena_h).addr, ARENA)

    open_psums = {}
    psum_order = []
    psum_uid = [0]

    def psopen(name, shape, dtype=F32):
        psum_uid[0] += 1
        cm = nc.psum_tensor(f"{name}_u{psum_uid[0]}", list(shape), dtype)
        h = cm.__enter__()
        open_psums[name] = cm
        psum_order.append(name)
        return h.ap()

    def psclose(*names):
        names = sorted(names, key=lambda n: psum_order.index(n), reverse=True)
        for n in names:
            assert psum_order[-1] == n, f"psum close order: {n} not on top {psum_order}"
            psum_order.pop()
            open_psums.pop(n).__exit__(None, None, None)

    with nc.Block() as block:
        P = Prog(nc, block)
        EV = P.events

        def on(engname):
            def deco(fn):
                P.on(engname, fn)
            return deco

        def barrier():
            snap = dict(P.cnt)
            for engname in Prog.ENGINES:
                def f(e, _s=snap):
                    e.barrier_wait(_s)
                P.on(engname, f)

        tapn = [0]

        def tap(name, ap, wait=None):
            if not DEBUG_TAPS:
                return
            tapn[0] += 1
            shape = list(ap.shape)
            t = nc.dram_tensor(f"dbg_{name}", shape, ap.dtype).ap()

            snap = {k: P.cnt[k] for k in ("vector", "scalar", "tensor", "gpsimd")}

            def f(e):
                if wait:
                    e.wait(wait)
                e.barrier_wait(snap)
                e.dma(e.dma_start(out=t, in_=ap), f"tap.{tapn[0]}", q="tap")
                e.wait(f"tap.{tapn[0]}")
            P.on("sync", f)

        C = {}
        for nme, shape, dt in [
            ("idf", [128, 128], F32), ("idh", [128, 128], F16), ("idb", [128, 128], BF16),
            ("selxb", [128, 12 * 24], F32), ("ones1", [1, 128], F32),
            ("ipb", [128, 24], F32), ("xpb", [128, 25], F32), ("opb", [128, 16], F32),
            ("convb", [128, 12], F32), ("convw", [128, 48], F32), ("nw", [128, 16], F32),
            ("c2", [24, 8], F32), ("rg", [1, 1], F32), ("rgrep", [128, 1], F32),
            ("hm1", [1, TH], F32), ("ep", [128, 4], F32), ("ep1", [24, 2], F32),
            ("onesc", [128, 1], F32),
        ]:
            C[nme] = al.alloc(nme, shape, dt)
        RW = {}
        for nme in ["mm", "r1", "cs2"]:
            RW[nme] = al.alloc("row_" + nme, [1, TH], F32)
        rA = al.alloc("ringA", [128, TH], F32)
        rB = al.alloc("ringB", [128, TH], F32)
        repA = al.alloc("repA", [128, TH], F32)
        repB = al.alloc("repB", [128, TH], F32)

        @on("sync")
        def _(e):
            for nme, src in [("idf", "idf"), ("selxb", "selxb"), ("ones1", "ones1"),
                             ("ipb", "ipb"), ("xpb", "xpb"), ("opb", "opb"),
                             ("convb", "convb"), ("convw", "convw"), ("nw", "nw"),
                             ("c2", "consts2"), ("rg", "rg"), ("hm1", "hmask1")]:
                e.dma(e.dma_start(out=C[nme], in_=din[src]), "ld." + nme, q="c0")
            P.events["ld.all"] = ("c0", P.cnt["c0"])

        @on("vector")
        def _(e):
            e.wait("ld.all")
            e.wait("ld.all")
            e.memset(C["ep"][:, 0:1], 1e-5)
            e.memset(C["ep"][:, 1:2], EPS32)
            e.memset(C["ep"][:, 2:3], 1.0)
            e.memset(C["ep"][:, 3:4], 0.0)
            e.memset(C["ep1"][:, 0:1], 1.0)
            e.memset(C["onesc"], 1.0)
            e.tensor_copy(C["idh"], C["idf"])
            e.mark(e.tensor_copy(C["idb"], C["idf"]), "p0.casts")

        x0r = [al.alloc(f"x0r{i}", [128, TH], F32) for i in range(2)]
        xnb = al.alloc("xnb", [128, KD * TH], BF16)
        NTH = nsplit(TH)
        pq = [psopen(f"pq{i}", [128, 512]) for i in range(2)]
        prm = [psopen(f"prm{i}", [1, 512]) for i in range(3)]
        prs = [psopen(f"prs{i}", [1, 512]) for i in range(3)]

        @on("tensor")
        def _(e):
            e.wait("p0.casts")
            e.wait("ld.all")
            e.wait("ld.all")
            e.mark(e.matmul(pq[0][:, 0:1], C["ones1"], C["rg"], start=True, stop=True), "A.rg")

        @on("vector")
        def _(e):
            e.wait("A.rg")
            e.mark(e.tensor_copy(C["rgrep"], pq[0][:, 0:1]), "A.rgrep")

        for k in range(KD):
            @on("sync")
            def _(e, k=k):
                if k >= 2:
                    e.wait(f"A.mm.{k-2}")
                e.dma(e.dma_start(out=x0r[k % 2], in_=x_in[128 * k:128 * (k + 1), :]), f"A.x0.{k}", q=f"ax{k % 2}")

            @on("scalar")
            def _(e, k=k):
                e.wait(f"A.x0.{k}")
                if k >= 1:
                    e.wait(f"A.mm.{k-1}")
                e.mark(e.activation(out=rA, in_=x0r[k % 2], func=AF.Square), f"A.sq.{k}")

            @on("tensor")
            def _(e, k=k):
                e.wait(f"A.sq.{k}")
                if k == 0:
                    e.wait("A.rgrep")
                i = None
                for ci, (c0, n) in enumerate(NTH):
                    e.matmul(prm[ci][:, 0:n], C["onesc"], x0r[k % 2][:, c0:c0 + n],
                             start=(k == 0), stop=(k == KD - 1))
                    i = e.matmul(prs[ci][:, 0:n], C["onesc"], rA[:, c0:c0 + n],
                                 start=(k == 0), stop=(k == KD - 1))
                e.mark(i, f"A.mm.{k}")

        @on("vector")
        def _(e):
            e.wait(f"A.mm.{KD-1}")
            for ci, (c0, n) in enumerate(NTH):
                e.tensor_scalar(out=RW["mm"][:, c0:c0 + n], in0=prm[ci][:, 0:n],
                                scalar1=1.0 / D, scalar2=None, op0=OP.mult)
                e.tensor_scalar(out=rA[0:1, c0:c0 + n], in0=prs[ci][:, 0:n],
                                scalar1=1.0 / D, scalar2=None, op0=OP.mult)
            e.mark(e.tensor_copy(rA[0:1, 0:1], rA[0:1, 0:1]), "A.r0")

        @on("scalar")
        def _(e):
            e.wait("A.r0")
            e.mark(e.activation(out=RW["cs2"], in_=RW["mm"], func=AF.Square), "A.r1")

        @on("vector")
        def _(e):
            e.wait("A.r1")
            e.mark(e.tensor_tensor(out=rB[0:1, :], in0=rA[0:1, :], in1=RW["cs2"], op=OP.subtract),
                   "A.r2")

        @on("scalar")
        def _(e):
            e.wait("A.r2")
            e.activation(out=RW["r1"], in_=rB[0:1, :], func=AF.Ln, scale=1.0, bias=C["ep"][0:1, 0:1])
            e.mark(e.activation(out=RW["r1"], in_=RW["r1"], func=AF.Exp, scale=-0.5), "A.r3")

        @on("vector")
        def _(e):
            e.wait("A.r3")
            e.tensor_tensor(out=RW["cs2"], in0=rB[0:1, :], in1=RW["r1"], op=OP.mult)
            e.mark(e.tensor_tensor(out=RW["cs2"], in0=RW["cs2"], in1=RW["r1"], op=OP.mult), "A.r4")

        @on("scalar")
        def _(e):
            e.wait("A.r4")
            e.activation(out=RW["cs2"], in_=RW["cs2"], func=AF.Ln, scale=1.0, bias=C["ep"][0:1, 1:2])
            e.mark(e.activation(out=RW["cs2"], in_=RW["cs2"], func=AF.Exp, scale=-0.5), "A.r5")

        @on("vector")
        def _(e):
            e.wait("A.r5")
            e.mark(e.tensor_tensor(out=RW["cs2"], in0=RW["cs2"], in1=RW["r1"], op=OP.mult), "A.rows")

        for ci, (c0, n) in enumerate(NTH):
            @on("tensor")
            def _(e, ci=ci, c0=c0, n=n):
                e.wait("A.rows")
                if ci >= 1:
                    e.wait(f"A.bcev.{ci-1}")
                e.matmul(pq[0][:, 0:n], C["ones1"], RW["mm"][:, c0:c0 + n], start=True, stop=True)
                e.mark(e.matmul(pq[1][:, 0:n], C["ones1"], RW["cs2"][:, c0:c0 + n],
                                start=True, stop=True), f"A.bc.{ci}")

            @on("vector")
            def _(e, ci=ci, c0=c0, n=n):
                e.wait(f"A.bc.{ci}")
                e.tensor_copy(repA[:, c0:c0 + n], pq[0][:, 0:n])
                e.mark(e.tensor_copy(repB[:, c0:c0 + n], pq[1][:, 0:n]), f"A.bcev.{ci}")

        for k in range(KD):
            @on("sync")
            def _(e, k=k):
                if k == 0:
                    e.wait(f"A.bcev.{len(NTH)-1}")
                if k >= 2:
                    e.wait(f"A.sp.{k-2}")
                e.dma(e.dma_start(out=x0r[k % 2], in_=x_in[128 * k:128 * (k + 1), :]), f"A.x0b.{k}", q=f"ax{k % 2}")

            @on("vector")
            def _(e, k=k):
                e.wait(f"A.x0b.{k}")
                e.tensor_tensor(out=rB, in0=x0r[k % 2], in1=repA, op=OP.subtract)
                e.tensor_tensor(out=rB, in0=rB, in1=repB, op=OP.mult)
                e.mark(e.tensor_scalar(out=xnb[:, k * TH:(k + 1) * TH], in0=rB,
                                       scalar1=C["nw"][:, k:k + 1], scalar2=None,
                                       op0=OP.mult), f"A.sp.{k}")

        tap("xnb", xnb)
        tap("mm", RW["mm"])
        tap("r1", RW["r1"])
        barrier()
        al.freed("x0r0", "x0r1", "row_cs2")
        psclose("prm0", "prm1", "prm2", "prs0", "prs1", "prs2")

        pg = [psopen(f"pg{i}", [128, 512]) for i in range(3)]
        wb16 = [al.alloc(f"wb16{i}", [128, KD * 128], BF16) for i in range(2)]
        xpart = al.alloc("xpart", [128, KI * TH], F32)
        szr = al.alloc("szr", [128, TO], F16)
        NZ = nsplit(TH, HALO)

        def oc_chunks_B(oc):
            return NTH if oc < 12 else NZ

        slotB = [0]
        for oc in range(24):
            @on("gpsimd")
            def _(e, oc=oc):
                if oc >= 2:
                    e.wait(f"B.mm.{oc-2}")
                e.dma(e.dma_start(out=wb16[oc % 2], in_=w_ip[oc]), f"B.wl.{oc}",
                      q=f"bw{oc % 2}")

            for ci, (c0, n) in enumerate(oc_chunks_B(oc)):
                s = slotB[0]

                @on("tensor")
                def _(e, oc=oc, ci=ci, c0=c0, n=n, s=s):
                    par = oc % 2
                    if ci == 0:
                        e.wait(f"B.wl.{oc}")
                    if s >= 3:
                        e.wait(f"B.ev.{s-3}")
                    bank = pg[s % 3]
                    i = None
                    for k in range(KD):
                        i = e.matmul(bank[:, 0:n], wb16[par][:, k * 128:(k + 1) * 128],
                                     xnb[:, k * TH + c0:k * TH + c0 + n],
                                     start=(k == 0), stop=(k == KD - 1))
                    e.mark(i, f"B.psum.{s}")
                    if ci == len(oc_chunks_B(oc)) - 1:
                        EV[f"B.mm.{oc}"] = EV[f"B.psum.{s}"]

                @on("scalar")
                def _(e, oc=oc, ci=ci, c0=c0, n=n, s=s):
                    e.wait(f"B.psum.{s}")
                    bank = pg[s % 3]
                    if oc < 12:
                        e.mark(e.activation(out=xpart[:, oc * TH + c0:oc * TH + c0 + n],
                                            in_=bank[:, 0:n], func=AF.Identity,
                                            bias=C["ipb"][:, oc:oc + 1]), f"B.ev.{s}")
                    else:
                        if ci == 0 and oc >= 13:
                            e.wait(f"B.szdma.{oc-13}")
                        e.mark(e.activation(out=szr[:, c0 - HALO:c0 - HALO + n],
                                            in_=bank[:, 0:n], func=AF.Silu,
                                            bias=C["ipb"][:, oc:oc + 1]), f"B.ev.{s}")
                        if ci == len(oc_chunks_B(oc)) - 1:
                            j = oc - 12
                            e.wait_cnt("scalar", P.cnt["scalar"])
                            e.dma(e.dma_start(out=silz_s[j * 128:(j + 1) * 128, :], in_=szr),
                                  f"B.szdma.{j}", q="bsz")
                slotB[0] += 1

        @on("tensor")
        def _(e):
            e.wait("B.ev.35")
            e.mark(e.matmul(pq[0][:, 0:HALO], C["ones1"], C["hm1"][:, 0:HALO], start=True, stop=True),
                   "B.hmrep")

        @on("vector")
        def _(e):
            e.wait("B.hmrep")
            i = None
            for j in range(KI):
                i = e.tensor_tensor(out=xpart[:, j * TH:j * TH + HALO],
                                    in0=xpart[:, j * TH:j * TH + HALO],
                                    in1=pq[0][:, 0:HALO], op=OP.mult)
            e.mark(i, "B.hm")

        tap("xpart", xpart)
        barrier()
        al.freed("xnb", "wb160", "wb161", "szr")
        psclose("pg0", "pg1", "pg2")

        xsb = al.alloc("xsb", [128, KI * TS_], BF16)
        xsT = al.alloc("xsT", [128, NCHK * DI], F16)
        x16s = al.alloc("x16s", [128, TS_], F16)
        pxb = [psopen(f"pxb{i}", [24, 512]) for i in range(3)]
        psT = psopen("psT", [128, 128], F16)
        NTS = nsplit(TS_)

        for j in range(KI):
            @on("vector")
            def _(e, j=j):
                base = j * TH
                if j >= 1:
                    e.wait(f"C.silu.{j-1}")
                e.tensor_scalar(out=rA[:, 0:TS_], in0=xpart[:, base:base + TS_],
                                scalar1=C["convw"][:, 4 * j:4 * j + 1], scalar2=None, op0=OP.mult)
                i = None
                for kk in range(1, 4):
                    i = e.scalar_tensor_tensor(out=rA[:, 0:TS_],
                                               in0=xpart[:, base + kk:base + kk + TS_],
                                               scalar=C["convw"][:, 4 * j + kk:4 * j + kk + 1],
                                               in1=rA[:, 0:TS_], op0=OP.mult, op1=OP.add)
                e.mark(i, f"C.conv.{j}")

            @on("scalar")
            def _(e, j=j):
                e.wait(f"C.conv.{j}")
                if j >= 1:
                    e.wait(f"C.x16.{j-1}")
                    e.wait(f"C.xb.{j-1}")
                e.mark(e.activation(out=rB[:, 0:TS_], in_=rA[:, 0:TS_], func=AF.Silu,
                                    bias=C["convb"][:, j:j + 1]), f"C.silu.{j}")

            @on("vector")
            def _(e, j=j):
                e.wait(f"C.silu.{j}")
                if j >= 1:
                    e.wait(f"C.tr.{j-1}.{NCHK-1}")
                e.tensor_copy(xsb[:, j * TS_:(j + 1) * TS_], rB[:, 0:TS_])
                e.mark(e.tensor_copy(x16s, rB[:, 0:TS_]), f"C.x16.{j}")

            @on("tensor")
            def _(e, j=j):
                e.wait(f"C.silu.{j}")
                i = None
                for ci, (c0, n) in enumerate(NTS):
                    i = e.matmul(pxb[ci][:, 0:n], C["selxb"][:, j * 24:(j + 1) * 24],
                                 rB[:, c0:c0 + n], start=(j == 0), stop=(j == KI - 1))
                e.mark(i, f"C.xb.{j}")

            for t in range(NCHK):
                @on("tensor")
                def _(e, j=j, t=t):
                    if t == 0:
                        e.wait(f"C.x16.{j}")
                    if not (j == 0 and t == 0):
                        pj, pt = (j, t - 1) if t > 0 else (j - 1, NCHK - 1)
                        e.wait(f"C.trev.{pj}.{pt}")
                    e.mark(e.transpose(psT, x16s[:, t * 128:(t + 1) * 128], C["idh"]), f"C.tr.{j}.{t}")

                @on("vector")
                def _(e, j=j, t=t):
                    e.wait(f"C.tr.{j}.{t}")
                    e.mark(e.tensor_copy(xsT[:, t * DI + j * 128:t * DI + (j + 1) * 128],
                                         psT), f"C.trev.{j}.{t}")

        tap("xsb", xsb)
        tap("xsT", xsT)
        barrier()
        al.freed("xpart", "x16s")
        psclose("psT")

        xbar = al.alloc("xbar", [24, TS_], F32)
        nB2 = al.alloc("nB2", [24, TS_], F32)
        nC2 = al.alloc("nC2", [24, TO], F32)
        dtr = al.alloc("dtr", [24, TS_], F32)

        @on("vector")
        def _(e):
            i = None
            for ci, (c0, n) in enumerate(NTS):
                i = e.tensor_scalar(out=xbar[:, c0:c0 + n], in0=pxb[ci][0:24, 0:n],
                                    scalar1=1.0 / 64, scalar2=None, op0=OP.mult)
            e.mark(i, "C.xbar")

        barrier()

        pg = [psopen(f"pg{i}", [128, 512]) for i in range(3)]
        wb16 = [al.alloc(f"wb16{i}", [128, KI * 128], BF16) for i in range(2)]
        b16 = al.alloc("b16", [128, KI * TS_], F16)
        c16 = al.alloc("c16", [128, KI * TO], F16)
        NBD = NTS
        NCD = [(128, 512), (640, 512)]

        def oc_chunks_D(oc):
            return NBD if (oc < 12 or oc == 24) else NCD

        evring = [rA, rB]
        sqring = [repA, repB]
        slotD = [0]
        for oc in range(25):
            @on("gpsimd")
            def _(e, oc=oc):
                if oc >= 2:
                    e.wait(f"D.mm.{oc-2}")
                if oc < 24:
                    e.dma(e.dma_start(out=wb16[oc % 2], in_=w_xp[oc]), f"D.wl.{oc}",
                          q=f"dw{oc % 2}")
                else:
                    e.dma(e.dma_start(out=wb16[oc % 2][:, 0:KI * 24], in_=w_xpd),
                          f"D.wl.{oc}", q=f"dw{oc % 2}")

            for ci, (c0, n) in enumerate(oc_chunks_D(oc)):
                s = slotD[0]
                M = 128 if oc < 24 else 24

                @on("tensor")
                def _(e, oc=oc, ci=ci, c0=c0, n=n, s=s, M=M):
                    par = oc % 2
                    if ci == 0:
                        e.wait(f"D.wl.{oc}")
                    if s >= 3:
                        e.wait(f"D.cast.{s-3}")
                        e.wait(f"D.nrm.{s-3}")
                    bank = pg[s % 3]
                    i = None
                    for k in range(KI):
                        i = e.matmul(bank[0:M, 0:n], wb16[par][:, k * M:(k + 1) * M],
                                     xsb[:, k * TS_ + c0:k * TS_ + c0 + n],
                                     start=(k == 0), stop=(k == KI - 1))
                    e.mark(i, f"D.psum.{s}")
                    if ci == len(oc_chunks_D(oc)) - 1:
                        EV[f"D.mm.{oc}"] = EV[f"D.psum.{s}"]

                @on("vector")
                def _(e, oc=oc, ci=ci, c0=c0, n=n, s=s, M=M):
                    e.wait(f"D.psum.{s}")
                    if s >= 2:
                        e.wait(f"D.cast.{s-2}")
                        e.wait(f"D.nrm.{s-2}")
                    bank = pg[s % 3]
                    bias = C["xpb"][:, oc:oc + 1] if oc < 24 else C["xpb"][0:24, 24:25]
                    e.scalar_tensor_tensor(out=evring[s % 2][0:M, 0:n], in0=bank[0:M, 0:n],
                                           scalar=bias, in1=sqring[s % 2][0:M, 0:n],
                                           op0=OP.add, op1=OP.bypass)
                    e.mark(e.tensor_tensor(out=sqring[s % 2][0:M, 0:n],
                                           in0=evring[s % 2][0:M, 0:n],
                                           in1=evring[s % 2][0:M, 0:n],
                                           op=OP.mult), f"D.ev.{s}")

                @on("vector")
                def _(e, oc=oc, ci=ci, c0=c0, n=n, s=s):
                    e.wait(f"D.ev.{s}")
                    if oc < 12:
                        i = e.tensor_copy(b16[:, oc * TS_ + c0:oc * TS_ + c0 + n],
                                          evring[s % 2][:, 0:n])
                    elif oc < 24:
                        j = oc - 12
                        i = e.tensor_copy(c16[:, j * TO + c0 - 128:j * TO + c0 - 128 + n],
                                          evring[s % 2][:, 0:n])
                    else:
                        i = e.tensor_copy(dtr[0:24, c0:c0 + n], evring[s % 2][0:24, 0:n])
                    e.mark(i, f"D.cast.{s}")

                @on("tensor")
                def _(e, oc=oc, ci=ci, c0=c0, n=n, s=s):
                    if oc < 12:
                        e.wait(f"D.ev.{s}")
                        e.mark(e.matmul(pxb[ci][:, 0:n], C["selxb"][:, oc * 24:(oc + 1) * 24],
                                        sqring[s % 2][:, 0:n], start=(oc == 0), stop=(oc == 11)),
                               f"D.nrm.{s}")
                    elif oc < 24:
                        e.wait(f"D.ev.{s}")
                        j = oc - 12
                        e.mark(e.matmul(pq[ci][0:24, 0:n], C["selxb"][:, j * 24:(j + 1) * 24],
                                        sqring[s % 2][:, 0:n], start=(j == 0), stop=(j == 11)),
                               f"D.nrm.{s}")
                    else:
                        EV[f"D.nrm.{s}"] = EV[f"D.cast.{s}"]
                slotD[0] += 1

        barrier()

        @on("vector")
        def _(e):
            i = None
            for ci, (c0, n) in enumerate(NTS):
                i = e.tensor_copy(nB2[:, c0:c0 + n], pxb[ci][0:24, 0:n])
            e.tensor_copy(nC2[:, 0:512], pq[0][0:24, 0:512])
            e.mark(e.tensor_copy(nC2[:, 512:1024], pq[1][0:24, 0:512]), "D.fin")

        tap("b16", b16)
        tap("c16raw", c16)
        tap("dtr", dtr)
        tap("nB2", nB2)
        tap("nC2", nC2)
        tap("xbar", xbar)
        barrier()
        al.freed("xsb", "wb160", "wb161")
        psclose("pg0", "pg1", "pg2", "pxb0", "pxb1", "pxb2")

        SC = {}
        SC["Lloc"] = al.alloc("Lloc", [24, TS_], F32)
        SC["dec"] = al.alloc("dec", [24, TS_], F32)
        SC["w2"] = al.alloc("w2", [24, TS_], F32)
        SC["sGT"] = al.alloc("sGT", [128, NCHK * 24], F32)
        SC["nLT"] = al.alloc("nLT", [128, NCHK * 24], F32)
        SC["dchR"] = al.alloc("dchR", [64, H * NCHK], F32)
        SC["SA"] = al.alloc("SA", [64, H * DH], F32)
        SC["SB"] = al.alloc("SB", [64, H * DH], F32)
        C["selh"] = al.alloc("selh", [24, 24 * 128], F32)
        C["sel64"] = al.alloc("sel64", [24, 24 * 64], F32)
        C["selpr"] = al.alloc("selpr", [24, 12 * 128], F32)
        C["maskp"] = al.alloc("maskp", [128, 128], F32)
        rr = [al.alloc(f"rr{i}", [24, TS_], F32) for i in range(3)]
        hc = {n: al.alloc("hc_" + n, [24, 1], F32) for n in ["clA", "ab", "omab", "beta", "emr"]}

        @on("sync")
        def _(e):
            e.dma(e.dma_start(out=C["selh"], in_=din["selh"]), "ld.selh", q="cE")
            e.dma(e.dma_start(out=C["sel64"], in_=din["sel64"]), "ld.sel64", q="cE")
            e.dma(e.dma_start(out=C["maskp"], in_=din["maskp"]), "ld.maskp", q="cE")
            e.dma(e.dma_start(out=C["selpr"], in_=din["selpr"]), "ld.selpr", q="cE")
            P.events["ld.cE"] = ("cE", P.cnt["cE"])

        @on("vector")
        def _(e):
            e.tensor_scalar(out=hc["clA"], in0=C["c2"][0:24, 0:1], scalar1=-2.3, scalar2=-0.01,
                            op0=OP.max, op1=OP.min)
            e.mark(e.tensor_scalar(out=hc["ab"], in0=C["c2"][0:24, 1:2], scalar1=-3.32,
                                   scalar2=-0.015, op0=OP.max, op1=OP.min), "E.v1")

        @on("scalar")
        def _(e):
            e.wait("E.v1")
            e.mark(e.activation(out=hc["ab"], in_=hc["ab"], func=AF.Exp, scale=LN2), "E.a1")

        @on("vector")
        def _(e):
            e.wait("E.a1")
            e.tensor_scalar(out=hc["ab"], in0=hc["ab"], scalar1=-1.0, scalar2=1.0,
                            op0=OP.mult, op1=OP.add)
            e.tensor_scalar(out=hc["omab"], in0=hc["ab"], scalar1=-1.0, scalar2=1.0,
                            op0=OP.mult, op1=OP.add)
            e.mark(e.tensor_scalar(out=hc["beta"], in0=C["c2"][0:24, 2:3], scalar1=-2.0,
                                   scalar2=2.0, op0=OP.max, op1=OP.min), "E.v2")

        @on("scalar")
        def _(e):
            e.wait("E.v2")
            e.mark(e.activation(out=hc["beta"], in_=hc["beta"], func=AF.Exp, scale=LN2), "E.a2")

        @on("vector")
        def _(e):
            e.wait("E.a2")
            e.tensor_scalar(out=hc["emr"], in0=C["c2"][0:24, 3:4], scalar1=1e-6, scalar2=None,
                            op0=OP.add)
            e.mark(e.reciprocal(out=hc["emr"], in_=hc["emr"]), "E.v3")

        @on("scalar")
        def _(e):
            e.wait("E.v3")
            e.activation(out=rr[0], in_=dtr, func=AF.Abs)
            e.activation(out=rr[0], in_=rr[0], func=AF.Exp, scale=-1.0)
            e.mark(e.activation(out=rr[0], in_=rr[0], func=AF.Ln, scale=1.0,
                                bias=C["ep1"][:, 0:1]), "E.a3")

        @on("vector")
        def _(e):
            e.wait("E.a3")
            e.tensor_scalar(out=rr[1], in0=dtr, scalar1=0.0, scalar2=None, op0=OP.max)
            e.tensor_tensor(out=rr[0], in0=rr[0], in1=rr[1], op=OP.add)
            e.mark(e.tensor_scalar(out=rr[0], in0=rr[0], scalar1=hc["clA"], scalar2=None,
                                   op0=OP.mult), "E.v4")

        al.freed("dtr")
        rr.append(al.alloc("rr3", [24, TS_], F32))

        @on("scalar")
        def _(e):
            e.wait("E.v4")
            e.activation(out=rr[1], in_=rr[0], func=AF.Exp)
            e.mark(e.activation(out=rr[2], in_=nB2, func=AF.Sqrt), "E.a4")

        @on("vector")
        def _(e):
            e.wait("E.a4")
            e.tensor_scalar(out=rr[2], in0=rr[2], scalar1=1e-12, scalar2=None, op0=OP.max)
            e.reciprocal(out=rr[2], in_=rr[2])
            e.tensor_scalar(out=rr[1], in0=rr[1], scalar1=-1.0, scalar2=1.0,
                            op0=OP.mult, op1=OP.add)
            e.tensor_tensor(out=rr[1], in0=rr[1], in1=xbar, op=OP.mult)
            e.tensor_tensor(out=rr[1], in0=rr[1], in1=rr[1], op=OP.mult)
            e.tensor_tensor(out=rr[3], in0=nB2, in1=rr[2], op=OP.mult)
            e.tensor_tensor(out=rr[3], in0=rr[3], in1=rr[2], op=OP.mult)
            e.tensor_tensor(out=rr[3], in0=rr[3], in1=rr[1], op=OP.mult)
            e.tensor_scalar(out=rr[3], in0=rr[3], scalar1=hc["emr"], scalar2=1.0 / 64,
                            op0=OP.mult, op1=OP.mult)
            e.mark(e.tensor_scalar(out=rr[3], in0=rr[3], scalar1=hc["beta"], scalar2=None,
                                   op0=OP.mult), "E.v5")
        @on("scalar")
        def _(e):
            e.wait("E.v5")
            e.activation(out=rr[3], in_=rr[3], func=AF.Tanh)
            e.mark(e.activation(out=rr[3], in_=rr[3], func=AF.Relu), "E.a5")

        @on("vector")
        def _(e):
            e.wait("E.a5")
            abb = hc["ab"].unsqueeze(-1).broadcast_to([24, 1, TS_])
            e.scalar_tensor_tensor(out=rr[3].rearrange("p (a b) -> p a b", a=1),
                                   in0=rr[3].rearrange("p (a b) -> p a b", a=1),
                                   scalar=hc["omab"], in1=abb, op0=OP.mult, op1=OP.add)
            e.tensor_scalar(out=rr[3], in0=rr[3], scalar1=0.01, scalar2=0.999,
                            op0=OP.max, op1=OP.min)
            e.mark(e.tensor_scalar(out=rr[1], in0=rr[3], scalar1=-1.0, scalar2=1.0,
                                   op0=OP.mult, op1=OP.add), "E.v6")
        @on("vector")
        def _(e):
            e.mark(e.tensor_tensor(out=rr[2], in0=rr[2], in1=rr[1], op=OP.mult), "E.v6b")

        @on("scalar")
        def _(e):
            e.wait("E.v6b")
            e.mark(e.activation(out=rr[1], in_=rr[3], func=AF.Ln), "E.a6")

        @on("vector")
        def _(e):
            e.wait("E.a6")
            for cc in range(NCHK):
                sl = slice(cc * Q, (cc + 1) * Q)
                e.tensor_tensor_scan(out=SC["Lloc"][:, sl], data0=rr[1][:, sl], data1=rr[1][:, sl],
                                     initial=0.0, op0=OP.add, op1=OP.bypass)
            lv = SC["Lloc"].rearrange("p (c q) -> p c q", q=Q)[:, :, Q - 1:Q]
            e.mark(e.tensor_copy(rr[1][:, 0:NCHK], lv.rearrange("p c q -> p (c q)")), "E.v7")

        @on("scalar")
        def _(e):
            e.wait("E.v7")
            e.mark(e.activation(out=SC["dec"], in_=SC["Lloc"], func=AF.Exp), "E.a7")

        @on("vector")
        def _(e):
            e.wait("E.a7")
            lsb = rr[1][:, 0:NCHK].unsqueeze(-1).broadcast_to([24, NCHK, Q])
            e.mark(e.tensor_tensor(out=SC["w2"].rearrange("p (c q) -> p c q", q=Q),
                                   in0=lsb, in1=SC["Lloc"].rearrange("p (c q) -> p c q", q=Q),
                                   op=OP.subtract), "E.v8")

        @on("scalar")
        def _(e):
            e.wait("E.v8")
            e.mark(e.activation(out=SC["w2"], in_=SC["w2"], func=AF.Exp), "E.a8")
            e.mark(e.activation(out=rr[1][:, 0:NCHK], in_=rr[1][:, 0:NCHK], func=AF.Exp), "E.dch")

        @on("vector")
        def _(e):
            e.wait("E.a8")
            e.mark(e.tensor_tensor(out=SC["w2"], in0=SC["w2"], in1=rr[2], op=OP.mult), "E.v9")

        @on("sync")
        def _(e):
            e.wait("E.dch")
            e.dma(e.dma_start(out=dch_s, in_=rr[1][:, 0:NCHK]), "E.dchw", q="edch")

        @on("tensor")
        def _(e):
            e.wait("E.v9")
            e.wait("ld.cE")
            e.wait("ld.cE")
            e.wait("ld.cE")
            e.mark(e.matmul(pq[0][0:24, 0:Q], C["ones1"][:, 0:24], C["hm1"][:, 3:3 + Q],
                            start=True, stop=True), "E.hmrep")

        @on("vector")
        def _(e):
            e.wait("E.hmrep")
            e.tensor_tensor(out=SC["w2"][:, 0:Q], in0=SC["w2"][:, 0:Q], in1=pq[0][0:24, 0:Q],
                            op=OP.mult)
            e.mark(e.tensor_scalar(out=rr[0], in0=SC["Lloc"], scalar1=-1.0, scalar2=None,
                                   op0=OP.mult), "E.v10")

        for cc in range(NCHK):
            @on("tensor")
            def _(e, cc=cc):
                if cc == 0:
                    e.wait("E.v10")
                else:
                    e.wait(f"E.trev.{cc-1}")
                e.transpose(pq[0][0:128, 0:24], rr[2][:, cc * Q:(cc + 1) * Q], C["idf"][0:24, 0:24])
                e.mark(e.transpose(pq[1][0:128, 0:24], rr[0][:, cc * Q:(cc + 1) * Q],
                                   C["idf"][0:24, 0:24]), f"E.tr.{cc}")

            @on("scalar")
            def _(e, cc=cc):
                e.wait(f"E.tr.{cc}")
                e.activation(out=SC["sGT"][:, cc * 24:(cc + 1) * 24], in_=pq[0][0:128, 0:24],
                             func=AF.Identity)
                e.mark(e.activation(out=SC["nLT"][:, cc * 24:(cc + 1) * 24], in_=pq[1][0:128, 0:24],
                                    func=AF.Identity), f"E.trev.{cc}")

        @on("sync")
        def _(e):
            e.wait("E.dchw")
            e.dma(e.dma_start(out=rr[1][0:1, 0:H * NCHK],
                              in_=dch_s.rearrange("a b -> (a b)").unsqueeze(0)), "E.dchr", q="edch")

        @on("tensor")
        def _(e):
            e.wait("E.dchr")
            e.wait(f"E.trev.{NCHK-1}")
            e.mark(e.matmul(pq[0][0:64, 0:H * NCHK], C["ones1"][:, 0:64], rr[1][0:1, 0:H * NCHK],
                            start=True, stop=True), "E.dchrep")

        @on("vector")
        def _(e):
            e.wait("E.dchrep")
            e.mark(e.tensor_copy(SC["dchR"], pq[0][0:64, 0:H * NCHK]), "E.dchR")

        @on("scalar")
        def _(e):
            e.wait("E.dchrep")
            e.mark(e.activation(out=rr[1][:, 0:TO], in_=nC2, func=AF.Sqrt), "E.cn1")

        @on("vector")
        def _(e):
            e.wait("E.cn1")
            e.tensor_scalar(out=rr[1][:, 0:TO], in0=rr[1][:, 0:TO], scalar1=1e-12, scalar2=None,
                            op0=OP.max)
            e.mark(e.reciprocal(out=rr[1][:, 0:TO], in_=rr[1][:, 0:TO]), "E.cn2")

        for j in range(KI):
            for half in range(2):
                idx = j * 2 + half
                c0, n = half * 512, 512

                @on("tensor")
                def _(e, j=j, half=half, idx=idx, c0=c0, n=n):
                    if idx == 0:
                        e.wait("E.cn2")
                        e.wait("ld.cE")
                        e.wait("E.dchR")
                    if idx >= 2:
                        e.wait(f"E.cnm.{idx-2}")
                    e.mark(e.matmul(pq[idx % 2][:, 0:n], C["selpr"][:, j * 128:(j + 1) * 128],
                                    rr[1][:, c0:c0 + n], start=True, stop=True), f"E.cnr.{idx}")

                @on("vector")
                def _(e, j=j, half=half, idx=idx, c0=c0, n=n):
                    e.wait(f"E.cnr.{idx}")
                    e.mark(e.tensor_tensor(out=c16[:, j * TO + c0:j * TO + c0 + n],
                                           in0=c16[:, j * TO + c0:j * TO + c0 + n],
                                           in1=pq[idx % 2][:, 0:n], op=OP.mult), f"E.cnm.{idx}")

        tap("alpha", rr[3], wait="E.v6")
        tap("Lloc", SC["Lloc"], wait="E.v7")
        tap("dec", SC["dec"], wait="E.a7")
        tap("w2", SC["w2"], wait="E.v10")
        tap("sGT", SC["sGT"], wait=f"E.trev.{NCHK-1}")
        tap("dchR", SC["dchR"], wait="E.dchR")
        tap("c16", c16, wait=f"E.cnm.{KI*2-1}")
        barrier()
        al.freed("xbar", "nB2", "nC2", "rr0", "rr1", "rr2", "rr3")
        for n_ in hc:
            al.freed("hc_" + n_)
        psclose("pq0", "pq1")

        ytm2 = [al.alloc(f"ytm{i}", [128, 4 * DI], F16) for i in range(2)]
        b2buf = [al.alloc(f"b2buf{i}", [64, 128], F16) for i in range(2)]
        b2t = [al.alloc(f"b2t{i}", [128, 64], F16) for i in range(2)]
        ebuf = [al.alloc(f"ebuf{i}", [128, 128], F32) for i in range(2)]
        w0buf = [al.alloc(f"w0buf{i}", [128, 128], F32) for i in range(2)]
        wgbuf = [al.alloc(f"wgbuf{i}", [128, 128], F16) for i in range(2)]
        cbuf = [al.alloc(f"cbuf{i}", [64, 128], F16) for i in range(2)]
        s16b = [al.alloc(f"s16b{i}", [64, DH], F16) for i in range(2)]

        psG = psopen("psG", [128, 128], F32)
        psL = psopen("psL", [128, 128], F32)
        psD = psopen("psD", [64, 128], F32)
        psW = psopen("psW", [64, 128], F32)
        psS = psopen("psS", [64, 64], F32)
        psT2 = psopen("psT2", [128, 64], F16)
        psY = [psopen(f"psY{i}", [128, 64], F32) for i in range(2)]

        @on("vector")
        def _(e):
            e.memset(SC["SA"], 0.0)
            e.mark(e.memset(SC["SB"], 0.0), "S.init")

        STEPS = [(cc, h) for cc in range(NCHK) for h in range(H)]
        lastown = {}
        prev_own_of = {}
        po = None
        for si, (cc, h) in enumerate(STEPS):
            prev_own_of[si] = po
            if cc >= 1:
                po = si

        for si, (cc, h) in enumerate(STEPS):
            j, hb = h // 2, 64 * (h % 2)
            own = cc >= 1
            oc0 = (cc - 1) * Q
            pr_ = si % 2
            Scur = SC["SA"] if (cc % 2 == 0) else SC["SB"]
            Snew = SC["SB"] if (cc % 2 == 0) else SC["SA"]

            @on("tensor")
            def _(e, si=si, cc=cc, h=h, j=j, hb=hb, own=own, oc0=oc0, pr_=pr_):
                if si == 0:
                    e.wait("S.init")
                if si >= 1:
                    e.waitif(f"S.wg.{si-1}")
                    e.wait(f"S.dvA.{si-1}")
                if own:
                    e.matmul(psG, b16[hb:hb + 64, j * TS_ + cc * Q:j * TS_ + (cc + 1) * Q],
                             c16[hb:hb + 64, j * TO + oc0:j * TO + oc0 + Q], start=True, stop=True)
                    e.matmul(psL, C["selh"][:, h * 128:(h + 1) * 128],
                             SC["Lloc"][:, cc * Q:(cc + 1) * Q], start=True, stop=True)
                    e.matmul(psD, C["sel64"][:, h * 64:(h + 1) * 64],
                             SC["dec"][:, cc * Q:(cc + 1) * Q], start=True, stop=True)
                e.mark(e.matmul(psW, C["sel64"][:, h * 64:(h + 1) * 64],
                                SC["w2"][:, cc * Q:(cc + 1) * Q], start=True, stop=True),
                       f"S.peA.{si}")

            @on("vector")
            def _(e, si=si, cc=cc, h=h, j=j, hb=hb, own=own, oc0=oc0, pr_=pr_,
                  Scur=Scur):
                e.wait(f"S.peA.{si}")
                if own:
                    if si >= 2:
                        e.waitif(f"S.wg.{si-2}")
                    e.waitif(f"S.w0.{si-2}")
                    e.tensor_tensor(out=ebuf[pr_], in0=C["maskp"], in1=psL, op=OP.add)
                    e.waitif(f"S.y.{si-2}")
                    e.tensor_copy(s16b[pr_], Scur[:, h * DH:(h + 1) * DH])
                e.waitif(f"S.tr.{si-2}")
                e.mark(e.tensor_tensor(out=b2buf[pr_],
                                       in0=b16[hb:hb + 64, j * TS_ + cc * Q:j * TS_ + (cc + 1) * Q],
                                       in1=psW, op=OP.mult), f"S.dvA.{si}")

            if own:
                @on("scalar")
                def _(e, si=si, cc=cc, h=h, pr_=pr_):
                    e.wait(f"S.dvA.{si}")
                    e.waitif(f"S.wg.{si-2}")
                    e.mark(e.activation(out=w0buf[pr_], in_=ebuf[pr_], func=AF.Exp,
                                        bias=SC["nLT"][:, cc * 24 + h:cc * 24 + h + 1]), f"S.w0.{si}")

                @on("vector")
                def _(e, si=si, cc=cc, h=h, j=j, hb=hb, oc0=oc0, pr_=pr_):
                    e.wait(f"S.w0.{si}")
                    e.waitif(f"S.y.{si-2}")
                    e.scalar_tensor_tensor(out=wgbuf[pr_], in0=psG,
                                           scalar=SC["sGT"][:, cc * 24 + h:cc * 24 + h + 1],
                                           in1=w0buf[pr_], op0=OP.mult, op1=OP.mult)
                    e.mark(e.tensor_tensor(out=cbuf[pr_],
                                           in0=c16[hb:hb + 64, j * TO + oc0:j * TO + oc0 + Q],
                                           in1=psD, op=OP.mult), f"S.wg.{si}")

            @on("tensor")
            def _(e, si=si, cc=cc, h=h, own=own, oc0=oc0, pr_=pr_):
                if own:
                    e.wait(f"S.wg.{si}")
                    e.waitif(f"S.yev.{si-2}")
                    e.matmul(psY[pr_], wgbuf[pr_], xsT[:, cc * DI + h * 64:cc * DI + (h + 1) * 64],
                             start=True, stop=False)
                    e.mark(e.matmul(psY[pr_], cbuf[pr_], s16b[pr_], start=False, stop=True),
                           f"S.y.{si}")
                e.wait(f"S.dvA.{si}")
                e.waitif(f"S.b2t.{si-1}")
                e.mark(e.transpose(psT2, b2buf[pr_], C["idh"][0:64, 0:64]), f"S.tr.{si}")

            @on("vector")
            def _(e, si=si, cc=cc, h=h, own=own, oc0=oc0, pr_=pr_):
                e.wait(f"S.tr.{si}")
                e.waitif(f"S.sc.{si-1}")
                e.mark(e.tensor_copy(b2t[pr_], psT2), f"S.b2t.{si}")
                if own:
                    e.wait(f"S.y.{si}")
                    ch = oc0 // Q
                    yb = ytm2[ch // 4]
                    yo = (ch % 4) * DI + h * 64
                    e.mark(e.tensor_copy(yb[:, yo:yo + 64], psY[pr_]), f"S.yev.{si}")

            @on("tensor")
            def _(e, si=si, cc=cc, h=h, pr_=pr_):
                e.wait(f"S.b2t.{si}")
                e.waitif(f"S.done.{si-1}")
                e.mark(e.matmul(psS, b2t[pr_], xsT[:, cc * DI + h * 64:cc * DI + (h + 1) * 64],
                                start=True, stop=True), f"S.sc.{si}")

            @on("vector")
            def _(e, si=si, cc=cc, h=h, pr_=pr_, Scur=Scur, Snew=Snew):
                e.wait(f"S.sc.{si}")
                e.mark(e.scalar_tensor_tensor(out=Snew[:, h * DH:(h + 1) * DH],
                                              in0=Scur[:, h * DH:(h + 1) * DH],
                                              scalar=SC["dchR"][:, h * NCHK + cc:h * NCHK + cc + 1],
                                              in1=psS, op0=OP.mult, op1=OP.add), f"S.done.{si}")

        tap("ytm0", ytm2[0])
        tap("ytm1", ytm2[1])
        tap("Send", SC["SB"] if NCHK % 2 == 1 else SC["SA"])
        barrier()
        al.freed("b16", "c16", "xsT", "Lloc", "dec", "w2", "sGT", "nLT", "dchR", "SA", "SB",
                 "selh", "sel64", "selpr", "maskp")
        for i in range(2):
            al.freed(f"b2buf{i}", f"b2t{i}", f"ebuf{i}", f"w0buf{i}", f"wgbuf{i}", f"cbuf{i}",
                     f"s16b{i}")
        psclose("psG", "psL", "psD", "psW", "psS", "psT2", "psY0", "psY1")

        rC = al.alloc("ringC", [128, TH], F32)
        ygb = al.alloc("ygb", [128, KI * TO], BF16)
        yfm = al.alloc("yfm", [128, TO], F16)
        szr2 = [al.alloc(f"szr2{i}", [128, TO], F16) for i in range(2)]
        psT3 = [psopen(f"psT3{i}", [128, 512], F16) for i in range(2)]

        for j in range(KI):
            @on("sync")
            def _(e, j=j):
                if j >= 2:
                    e.wait(f"F.g.{j-2}")
                e.dma(e.dma_start(out=szr2[j % 2], in_=silz_s[j * 128:(j + 1) * 128, :]), f"F.sz.{j}", q=f"fsz{j % 2}")

            for t in range(8):
                g, s4 = t // 4, t % 4
                @on("tensor")
                def _(e, j=j, t=t, g=g, s4=s4):
                    if s4 == 0 and j >= 1:
                        e.wait(f"F.trev.{j-1}.{g}")
                    yb = ytm2[t // 4]
                    yo = (t % 4) * DI + j * 128
                    e.mark(e.transpose(psT3[g][:, s4 * 128:(s4 + 1) * 128], yb[:, yo:yo + 128],
                                       C["idh"]), f"F.tr.{j}.{t}")

                if s4 == 3:
                    @on("vector")
                    def _(e, j=j, t=t, g=g):
                        e.wait(f"F.tr.{j}.{t}")
                        if g == 0 and j >= 1:
                            e.wait(f"F.g.{j-1}")
                        e.mark(e.tensor_copy(yfm[:, g * 512:(g + 1) * 512], psT3[g]),
                               f"F.trev.{j}.{g}")

            @on("vector")
            def _(e, j=j):
                e.wait(f"F.trev.{j}.1")
                e.wait(f"F.sz.{j}")
                e.mark(e.tensor_tensor(out=ygb[:, j * TO:(j + 1) * TO], in0=yfm,
                                       in1=szr2[j % 2], op=OP.mult), f"F.g.{j}")

        tap("ygb", ygb)
        barrier()
        al.freed("ytm0", "ytm1", "yfm", "szr20", "szr21")
        psclose("psT30", "psT31")

        pq = [psopen(f"pq{i}", [128, 512]) for i in range(2)]
        NT2 = nsplit(TO)

        for ci, (c0, n) in enumerate(NT2):
            @on("tensor")
            def _(e, ci=ci, c0=c0, n=n):
                if ci >= 1:
                    e.wait(f"F.bce.{ci-1}")
                e.matmul(pq[0][:, 0:n], C["ones1"], RW["mm"][:, HALO + c0:HALO + c0 + n],
                         start=True, stop=True)
                e.mark(e.matmul(pq[1][:, 0:n], C["ones1"], RW["r1"][:, HALO + c0:HALO + c0 + n],
                                start=True, stop=True), f"F.bc.{ci}")

            @on("vector")
            def _(e, ci=ci, c0=c0, n=n):
                e.wait(f"F.bc.{ci}")
                e.tensor_copy(repA[:, c0:c0 + n], pq[0][:, 0:n])
                e.mark(e.tensor_copy(repB[:, c0:c0 + n], pq[1][:, 0:n]), f"F.bce.{ci}")

        psclose("pq0", "pq1")
        pg = [psopen(f"pg{i}", [128, 512]) for i in range(3)]
        pr = [psopen(f"pr{i}", [1, 512]) for i in range(2)]
        pr2 = [psopen(f"pr2{i}", [1, 512]) for i in range(2)]
        wb16 = [al.alloc(f"wb16{i}", [128, KI * 128], BF16) for i in range(2)]
        x0c = [al.alloc(f"x0c{i}", [128, TO], F32) for i in range(2)]
        x1r = [al.alloc(f"x1r{i}", [128, 512], F32) for i in range(2)]
        evF = [rA, rC]

        slotF = [0]
        for oc in range(16):
            @on("gpsimd")
            def _(e, oc=oc):
                if oc >= 2:
                    e.wait(f"F.mm.{oc-2}")
                e.dma(e.dma_start(out=wb16[oc % 2], in_=w_op[oc]), f"F.wl.{oc}",
                      q=f"fw{oc % 2}")

            @on("sync")
            def _(e, oc=oc):
                if oc >= 2:
                    e.wait(f"F.resoc.{oc-2}")
                e.dma(e.dma_start(out=x0c[oc % 2], in_=x_in[128 * oc:128 * (oc + 1), HALO:]),
                      f"F.x0.{oc}", q=f"fx{oc % 2}")

            for ci, (c0, n) in enumerate(NT2):
                s = slotF[0]

                @on("tensor")
                def _(e, oc=oc, ci=ci, c0=c0, n=n, s=s):
                    par = oc % 2
                    if ci == 0:
                        e.wait(f"F.wl.{oc}")
                        if oc == 0:
                            e.wait(f"F.bce.{len(NT2)-1}")
                    if s >= 3:
                        e.wait(f"F.ev.{s-3}")
                    bank = pg[s % 3]
                    i = None
                    for k in range(KI):
                        i = e.matmul(bank[:, 0:n], wb16[par][:, k * 128:(k + 1) * 128],
                                     ygb[:, k * TO + c0:k * TO + c0 + n],
                                     start=(k == 0), stop=(k == KI - 1))
                    e.mark(i, f"F.psum.{s}")
                    if ci == len(NT2) - 1:
                        EV[f"F.mm.{oc}"] = EV[f"F.psum.{s}"]

                @on("vector")
                def _(e, oc=oc, ci=ci, c0=c0, n=n, s=s):
                    e.wait(f"F.psum.{s}")
                    if s >= 2:
                        e.wait(f"F.st.{s-2}")
                        e.wait(f"F.x1w.{s-2}")
                    e.mark(e.scalar_tensor_tensor(out=evF[s % 2][:, 0:n], in0=pg[s % 3][:, 0:n],
                                                  scalar=C["opb"][:, oc:oc + 1],
                                                  in1=x1r[s % 2][:, 0:n],
                                                  op0=OP.add, op1=OP.bypass), f"F.ev.{s}")

                @on("vector")
                def _(e, oc=oc, ci=ci, c0=c0, n=n, s=s):
                    e.wait(f"F.ev.{s}")
                    e.wait(f"F.x0.{oc}")
                    xs0 = x0c[oc % 2][:, c0:c0 + n]
                    e.tensor_tensor(out=rB[:, 0:n], in0=xs0, in1=repA[:, c0:c0 + n], op=OP.subtract)
                    e.tensor_tensor(out=rB[:, 0:n], in0=rB[:, 0:n], in1=repB[:, c0:c0 + n],
                                    op=OP.mult)
                    e.tensor_tensor(out=rB[:, 0:n], in0=rB[:, 0:n], in1=xs0, op=OP.add)
                    e.mark(e.scalar_tensor_tensor(out=x1r[s % 2][:, 0:n], in0=evF[s % 2][:, 0:n],
                                                  scalar=C["rgrep"], in1=rB[:, 0:n],
                                                  op0=OP.mult, op1=OP.add), f"F.res.{s}")
                    if ci == len(NT2) - 1:
                        EV[f"F.resoc.{oc}"] = EV[f"F.res.{s}"]

                @on("scalar")
                def _(e, oc=oc, ci=ci, c0=c0, n=n, s=s):
                    e.wait(f"F.res.{s}")
                    e.dma(e.dma_start(out=x1_s[oc * 128:(oc + 1) * 128, c0:c0 + n],
                                      in_=x1r[s % 2][:, 0:n]), f"F.x1w.{s}", q=f"fxw{s % 2}")

                @on("vector")
                def _(e, oc=oc, ci=ci, c0=c0, n=n, s=s):
                    e.mark(e.tensor_tensor(out=evF[s % 2][:, 0:n], in0=x1r[s % 2][:, 0:n],
                                           in1=x1r[s % 2][:, 0:n], op=OP.mult), f"F.sq.{s}")

                @on("tensor")
                def _(e, oc=oc, ci=ci, c0=c0, n=n, s=s):
                    e.wait(f"F.sq.{s}")
                    e.matmul(pr[ci][:, 0:n], C["onesc"], x1r[s % 2][:, 0:n],
                             start=(oc == 0), stop=(oc == 15))
                    e.mark(e.matmul(pr2[ci][:, 0:n], C["onesc"], evF[s % 2][:, 0:n],
                                    start=(oc == 0), stop=(oc == 15)), f"F.st.{s}")
                slotF[0] += 1

        barrier()
        tap("rm2pre", rA, wait=None) if False else None
        RW["m2"] = al.alloc("row_m2", [1, TO], F32)
        RW["r2"] = al.alloc("row_r2", [1, TO], F32)

        @on("vector")
        def _(e):
            for ci, (c0, n) in enumerate(NT2):
                e.tensor_scalar(out=RW["m2"][:, c0:c0 + n], in0=pr[ci][:, 0:n],
                                scalar1=1.0 / D, scalar2=None, op0=OP.mult)
                e.tensor_scalar(out=RW["r2"][:, c0:c0 + n], in0=pr2[ci][:, 0:n],
                                scalar1=1.0 / D, scalar2=None, op0=OP.mult)
            e.mark(e.tensor_copy(rA[0:1, 0:1], rA[0:1, 0:1]), "G.r0")

        @on("scalar")
        def _(e):
            e.wait("G.r0")
            e.mark(e.activation(out=rA[0:1, 0:TO], in_=RW["m2"], func=AF.Square), "G.r1")

        @on("vector")
        def _(e):
            e.wait("G.r1")
            e.mark(e.tensor_tensor(out=RW["r2"], in0=RW["r2"], in1=rA[0:1, 0:TO], op=OP.subtract),
                   "G.r2")

        @on("scalar")
        def _(e):
            e.wait("G.r2")
            e.activation(out=RW["r2"], in_=RW["r2"], func=AF.Ln, scale=1.0, bias=C["ep"][0:1, 0:1])
            e.mark(e.activation(out=RW["r2"], in_=RW["r2"], func=AF.Exp, scale=-0.5), "G.r3")

        tap("m2", RW["m2"], wait="G.r3")
        tap("r2", RW["r2"], wait="G.r3")
        barrier()
        al.freed("ygb", "wb160", "wb161", "x0c0", "x0c1",
                 "x1r0", "x1r1", "row_mm", "row_r1")
        psclose("pg0", "pg1", "pg2", "pr0", "pr1", "pr20", "pr21")

        l2b = al.alloc("l2b", [128, KD * TT], BF16)
        hb = al.alloc("hb", [128, KF * TT], BF16)
        wnat = [al.alloc(f"wnat{i}", [128, 4096], BF16) for i in range(2)]
        qr = [al.alloc(f"qr{i}", [128, 4096], BF16) for i in range(2)]
        qdd = al.alloc("qdd", [128, 4096], BF16)
        qsc = al.alloc("qsc", [128, 64], BF16)
        qscf = al.alloc("qscf", [128, 64], F32)
        qsum = al.alloc("qsum", [128, 64], F32)
        lhsT1 = [al.alloc(f"lhsT1{i}", [128, KD * 128], BF16) for i in range(2)]
        _l2 = al.alloc("lhsT2", [128, KF * 128], BF16)
        lhsT2 = [_l2, _l2]
        gx = [al.alloc(f"gx{i}", [128, TT], F32) for i in range(2)]
        rrow = al.alloc("rrow", [1, TT], F32)
        rrep = al.alloc("rrep", [128, TT], F32)

        psF = [psopen(f"psF{i}", [128, TT]) for i in range(4)]
        psH = psopen("psH", [1, TT])
        psQT = [psopen(f"psQT{i}", [128, 512], BF16) for i in range(2)]
        pq = [psopen(f"pq{i}", [128, 512]) for i in range(1)]

        for ci, (c0, n) in enumerate(NT2):
            @on("tensor")
            def _(e, ci=ci, c0=c0, n=n):
                e.wait("G.r3")
                if ci >= 1:
                    e.wait(f"G.bce.{ci-1}")
                e.mark(e.matmul(pq[0][:, 0:n], C["ones1"], RW["m2"][:, c0:c0 + n],
                                start=True, stop=True), f"G.bcm.{ci}")

            @on("vector")
            def _(e, ci=ci, c0=c0, n=n):
                e.wait(f"G.bcm.{ci}")
                e.mark(e.tensor_copy(repA[:, c0:c0 + n], pq[0][:, 0:n]), f"G.bcme.{ci}")

            @on("tensor")
            def _(e, ci=ci, c0=c0, n=n):
                e.wait(f"G.bcme.{ci}")
                e.mark(e.matmul(pq[0][:, 0:n], C["ones1"], RW["r2"][:, c0:c0 + n],
                                start=True, stop=True), f"G.bcr.{ci}")

            @on("vector")
            def _(e, ci=ci, c0=c0, n=n):
                e.wait(f"G.bcr.{ci}")
                e.mark(e.tensor_copy(repB[:, c0:c0 + n], pq[0][:, 0:n]), f"G.bce.{ci}")

        tap("grepA", repA, wait=f"G.bce.1")
        tap("grepB", repB, wait=f"G.bce.1")

        def qunit(u):
            if u < 128:
                kk, gi = u // 2, u % 2
                oc = kk if gi == 0 else KF + kk
                src = w_gu_nat[oc * 128:(oc + 1) * 128, :]
                ncol, ngrp, ntr = D, 32, KD
                dst = lhsT1[u % 2]
                dst_off = 0
                scr = wqgu_s[oc * 128:(oc + 1) * 128, :]
                mmlabel = f"G0.w1mm.{u-2}"
                wslabel = f"G0.ws.{u-2}"
            else:
                v = u - 128
                oc, half = v // 2, v % 2
                src = w_pr_nat[oc * 128:(oc + 1) * 128, half * 4096:(half + 1) * 4096]
                ncol, ngrp, ntr = 4096, 64, 32
                dst = lhsT2[oc % 2]
                dst_off = half * 32
                scr = wqpr_s[oc * 128:(oc + 1) * 128, half * 4096:(half + 1) * 4096]
                mmlabel = f"G0.w2mm.{oc-1}"
                wslabel = f"G0.ws.{u-2}"
            par = u % 2

            @on("gpsimd")
            def _(e, u=u, src=src, ncol=ncol, par=par):
                if u >= 2:
                    e.wait(f"G0.wq.{u-2}")
                e.dma(e.dma_start(out=wnat[par][:, 0:ncol], in_=src), f"G0.wn.{u}", q=f"gn{u % 2}")

            @on("vector")
            def _(e, u=u, ncol=ncol, ngrp=ngrp, ntr=ntr, par=par):
                e.wait(f"G0.wn.{u}")
                if u >= 2:
                    e.waitif(f"G0.qtrlast.{u-2}")
                e.scalar_tensor_tensor(out=qr[par][:, 0:ncol], in0=wnat[par][:, 0:ncol],
                                       scalar=-1.0, in1=wnat[par][:, 0:ncol],
                                       op0=OP.mult, op1=OP.max)
                e.tensor_reduce(out=qsum[:, 0:ngrp],
                                in_=qr[par][:, 0:ncol].rearrange("p (g v) -> p g v", v=64),
                                axis=AX.X, op=OP.add)
                e.tensor_scalar(out=qsc[:, 0:ngrp], in0=qsum[:, 0:ngrp], scalar1=1.0 / 64,
                                scalar2=None, op0=OP.mult)
                e.tensor_scalar(out=qsc[:, 0:ngrp], in0=qsc[:, 0:ngrp], scalar1=1e-8,
                                scalar2=None, op0=OP.max)
                e.mark(e.reciprocal(out=qscf[:, 0:ngrp], in_=qsc[:, 0:ngrp]), f"G0.sc.{u}")
                rsb = qscf[:, 0:ngrp].unsqueeze(-1).broadcast_to([128, ngrp, 64])
                scb = qsc[:, 0:ngrp].unsqueeze(-1).broadcast_to([128, ngrp, 64])
                qv = qr[par][:, 0:ncol].rearrange("p (g v) -> p g v", v=64)
                wv = wnat[par][:, 0:ncol].rearrange("p (g v) -> p g v", v=64)
                e.tensor_tensor(out=qv, in0=wv, in1=rsb, op=OP.mult)
                e.tensor_scalar(out=qr[par][:, 0:ncol], in0=qr[par][:, 0:ncol],
                                scalar1=-1.0, scalar2=1.0, op0=OP.max, op1=OP.min)
                e.tensor_scalar(out=qr[par][:, 0:ncol], in0=qr[par][:, 0:ncol],
                                scalar1=MAGIC, scalar2=-MAGIC, op0=OP.add, op1=OP.add)
                e.tensor_tensor(out=qv, in0=qv, in1=scb, op=OP.mult)
                e.tensor_tensor(out=qdd[:, 0:ncol], in0=qr[par][:, 0:ncol],
                                in1=wnat[par][:, 0:ncol], op=OP.subtract)
                e.mark(e.tensor_tensor(out=qr[par][:, 0:ncol], in0=qdd[:, 0:ncol],
                                       in1=wnat[par][:, 0:ncol], op=OP.add), f"G0.wq.{u}")

            for k in range(ntr):
                g, s4 = k // 4, k % 4
                @on("tensor")
                def _(e, u=u, k=k, g=g, s4=s4, par=par, ntr=ntr):
                    if k == 0:
                        e.wait(f"G0.wq.{u}")
                    if g < 2 and s4 == 0:
                        e.waitif(f"G0.qteL{g}.{u-1}")
                    if g >= 2 and s4 == 0:
                        e.wait(f"G0.qte.{u}.{g-2}")
                    e.mark(e.transpose(psQT[g % 2][:, s4 * 128:(s4 + 1) * 128],
                                       qr[par][:, k * 128:(k + 1) * 128], C["idb"]),
                           f"G0.qtr.{u}.{k}")

                if s4 == 3:
                    @on("vector")
                    def _(e, u=u, k=k, g=g, par=par, dst=dst, dst_off=dst_off,
                          mmlabel=mmlabel, wslabel=wslabel, ntr=ntr):
                        e.wait(f"G0.qtr.{u}.{k}")
                        if g == 0:
                            e.waitif(mmlabel)
                            e.waitif(wslabel)
                        e.mark(e.tensor_copy(dst[:, (dst_off + g * 4) * 128:(dst_off + g * 4 + 4) * 128],
                                             psQT[g % 2]), f"G0.qte.{u}.{g}")

            ng = ntr // 4
            EV[f"G0.qtrlast.{u}"] = EV[f"G0.qtr.{u}.{ntr-1}"]
            EV[f"G0.qteL0.{u}"] = EV[f"G0.qte.{u}.{ng-2}"]
            EV[f"G0.qteL1.{u}"] = EV[f"G0.qte.{u}.{ng-1}"]

            @on("scalar")
            def _(e, u=u, scr=scr, dst=dst, dst_off=dst_off, ntr=ntr):
                e.wait(f"G0.qte.{u}.{ntr // 4 - 1}")
                e.wait_cnt("scalar", P.cnt["scalar"])
                e.dma(e.dma_start(out=scr, in_=dst[:, dst_off * 128:(dst_off + ntr) * 128]),
                      f"G0.ws.{u}", q=("gwsa" + str(u % 2)) if u < 128 else ("gwsb" + str(u % 2)))

        for tt in range(NTT):
            for k in range(KD):
                @on("sync")
                def _(e, tt=tt, k=k):
                    if tt == 0 and k == 0:
                        e.wait(f"G.bce.{len(NT2)-1}")
                    if tt > 0 and k == 0:
                        e.wait(f"G{tt-1}.lastmm")
                    if tt > 0 and k < 2:
                        e.wait(f"G{tt-1}.out.{14 + k}")
                    if k >= 2:
                        e.wait(f"G{tt}.l2.{k-2}")
                    e.dma(e.dma_start(out=gx[k % 2], in_=x1_s[128 * k:128 * (k + 1),
                                                           tt * TT:(tt + 1) * TT]), f"G{tt}.x1.{k}", q=f"ggx{k % 2}")

                @on("vector")
                def _(e, tt=tt, k=k):
                    e.wait(f"G{tt}.x1.{k}")
                    if tt > 0 and k == 0:
                        e.wait(f"G{tt-1}.lastmm")
                    e.tensor_tensor(out=rA[:, 0:TT], in0=gx[k % 2],
                                    in1=repA[:, tt * TT:(tt + 1) * TT], op=OP.subtract)
                    e.mark(e.tensor_tensor(out=l2b[:, k * TT:(k + 1) * TT], in0=rA[:, 0:TT],
                                           in1=repB[:, tt * TT:(tt + 1) * TT], op=OP.mult),
                           f"G{tt}.l2.{k}")

            for kk in range(KF):
                for gi in range(2):
                    u = kk * 2 + gi
                    if tt == 0:
                        qunit(u)
                    else:
                        @on("sync")
                        def _(e, tt=tt, u=u, kk=kk, gi=gi):
                            oc = kk if gi == 0 else KF + kk
                            if u >= 2:
                                e.wait(f"G{tt}.w1mm.{u-2}")
                            e.dma(e.dma_start(out=lhsT1[u % 2],
                                              in_=wqgu_s[oc * 128:(oc + 1) * 128, :]),
                                  f"G{tt}.w1l.{u}", q=f"gw1{u % 2}")

                    @on("tensor")
                    def _(e, tt=tt, u=u, kk=kk, gi=gi):
                        par = u % 2
                        if tt == 0:
                            e.wait(f"G0.qte.{u}.{KD // 4 - 1}")
                        else:
                            e.wait(f"G{tt}.w1l.{u}")
                        if kk == 0 and gi == 0:
                            e.wait(f"G{tt}.l2.{KD-1}")
                        if kk >= 2:
                            e.wait(f"G{tt}.h.{kk-2}")
                        pbank = psF[kk % 2] if gi == 0 else psF[2 + kk % 2]
                        i = None
                        for k in range(KD):
                            i = e.matmul(pbank, lhsT1[par][:, k * 128:(k + 1) * 128],
                                         l2b[:, k * TT:(k + 1) * TT],
                                         start=(k == 0), stop=(k == KD - 1))
                        e.mark(i, f"G{tt}.w1mm.{u}")

                @on("scalar")
                def _(e, tt=tt, kk=kk):
                    e.wait(f"G{tt}.w1mm.{kk*2}")
                    e.wait(f"G{tt}.w1mm.{kk*2+1}")
                    if kk >= 1:
                        e.wait(f"G{tt}.st.{kk-1}")
                    e.mark(e.activation(out=rA[:, 0:TT], in_=psF[kk % 2], func=AF.Silu),
                           f"G{tt}.sg.{kk}")

                @on("vector")
                def _(e, tt=tt, kk=kk):
                    e.wait(f"G{tt}.sg.{kk}")
                    if kk == 0 and tt > 0:
                        e.wait(f"G{tt-1}.y.14")
                    if kk >= 1:
                        e.wait(f"G{tt}.st.{kk-1}")
                    e.tensor_tensor(out=rB[:, 0:TT], in0=rA[:, 0:TT], in1=psF[2 + kk % 2],
                                    op=OP.mult)
                    e.mark(e.tensor_copy(hb[:, kk * TT:(kk + 1) * TT], rB[:, 0:TT]),
                           f"G{tt}.h.{kk}")

                @on("vector")
                def _(e, tt=tt, kk=kk):
                    e.wait(f"G{tt}.h.{kk}")
                    e.mark(e.tensor_tensor(out=rA[:, 0:TT], in0=rB[:, 0:TT],
                                           in1=rB[:, 0:TT], op=OP.mult),
                           f"G{tt}.hsq.{kk}")

                @on("tensor")
                def _(e, tt=tt, kk=kk):
                    e.wait(f"G{tt}.hsq.{kk}")
                    e.mark(e.matmul(psH, C["onesc"], rA[:, 0:TT], start=(kk == 0),
                                    stop=(kk == KF - 1)), f"G{tt}.st.{kk}")

            @on("scalar")
            def _(e, tt=tt):
                e.wait(f"G{tt}.st.{KF-1}")
                e.activation(out=rrow, in_=psH, func=AF.Identity, scale=1.0 / FF)
                e.activation(out=rrow, in_=rrow, func=AF.Ln, scale=1.0, bias=C["ep"][0:1, 1:2])
                e.mark(e.activation(out=rrow, in_=rrow, func=AF.Exp, scale=-0.5), f"G{tt}.rr")

            @on("tensor")
            def _(e, tt=tt):
                e.wait(f"G{tt}.rr")
                if tt >= 1:
                    e.wait(f"G{tt-1}.rrepev")
                e.mark(e.matmul(pq[0][:, 0:TT], C["ones1"], rrow, start=True, stop=True),
                       f"G{tt}.rrb")

            @on("vector")
            def _(e, tt=tt):
                e.wait(f"G{tt}.rrb")
                e.mark(e.tensor_copy(rrep, pq[0][:, 0:TT]), f"G{tt}.rrepev")

            if tt == 0:
                tap("rrow", rrow, wait="G0.rr")
                tap("rrep", rrep, wait="G0.rrepev")
                tap("hb", hb, wait=f"G0.h.{KF-1}")
                tap("l2b", l2b, wait=f"G0.l2.{KD-1}")

            for oc in range(16):
                if tt == 0:
                    qunit(128 + oc * 2 + 0)
                    qunit(128 + oc * 2 + 1)
                else:
                    @on("sync")
                    def _(e, tt=tt, oc=oc):
                        if tt == 1 and oc == 0:
                            e.wait("G0.ws.158")
                            e.wait("G0.ws.159")
                        if oc >= 1:
                            e.wait(f"G{tt}.o.{oc-1}")
                        e.dma(e.dma_start(out=lhsT2[oc % 2],
                                          in_=wqpr_s[oc * 128:(oc + 1) * 128, :]), f"G{tt}.w2l.{oc}", q="gw2")

                @on("sync")
                def _(e, tt=tt, oc=oc):
                    if oc >= 2:
                        e.wait(f"G{tt}.out.{oc-2}")
                    else:
                        e.wait(f"G{tt}.l2.{KD-2+oc}")
                    e.dma(e.dma_start(out=gx[oc % 2],
                                      in_=x1_s[oc * 128:(oc + 1) * 128, tt * TT:(tt + 1) * TT]),
                          f"G{tt}.x1b.{oc}", q=f"ggx{oc % 2}")

                @on("tensor")
                def _(e, tt=tt, oc=oc):
                    par = oc % 2
                    if tt == 0:
                        e.wait(f"G0.qte.{128+oc*2+1}.{7}")
                    else:
                        e.wait(f"G{tt}.w2l.{oc}")
                    if oc == 0:
                        e.wait(f"G{tt}.h.{KF-1}")
                    if oc >= 1:
                        e.wait(f"G{tt}.out.{oc-1}")
                    i = None
                    for k in range(KF):
                        i = e.matmul(psF[0], lhsT2[par][:, k * 128:(k + 1) * 128],
                                     hb[:, k * TT:(k + 1) * TT],
                                     start=(k == 0), stop=(k == KF - 1))
                    e.mark(i, f"G{tt}.o.{oc}")
                    if tt == 0:
                        EV[f"G0.w2mm.{oc}"] = EV[f"G{tt}.o.{oc}"]

                @on("vector")
                def _(e, tt=tt, oc=oc):
                    e.wait(f"G{tt}.o.{oc}")
                    e.wait(f"G{tt}.x1b.{oc}")
                    e.wait(f"G{tt}.rrepev")
                    if oc >= 2:
                        e.wait(f"G{tt}.y.{oc-2}")
                    elif oc == 1 and tt > 0:
                        e.wait(f"G{tt-1}.y.15")
                    ring = rB if oc % 2 == 0 else rC
                    e.tensor_tensor(out=ring[:, 0:TT], in0=psF[0], in1=rrep, op=OP.mult)
                    e.mark(e.tensor_tensor(out=ring[:, 0:TT], in0=ring[:, 0:TT],
                                           in1=gx[oc % 2], op=OP.add), f"G{tt}.out.{oc}")

                @on("scalar")
                def _(e, tt=tt, oc=oc):
                    e.wait(f"G{tt}.out.{oc}")
                    ring = rB if oc % 2 == 0 else rC
                    e.dma(e.dma_start(out=y_out[oc * 128:(oc + 1) * 128, tt * TT:(tt + 1) * TT],
                                      in_=ring[:, 0:TT]), f"G{tt}.y.{oc}", q=f"gy{oc % 2}")

            EV[f"G{tt}.lastmm"] = EV[f"G{tt}.o.15"]

        @on("sync")
        def _(e):
            e.wait(f"G{NTT-1}.y.14")
            e.wait(f"G{NTT-1}.y.15")
            e.wait_cnt("gy0", P.cnt["gy0"])
            e.wait_cnt("gy1", P.cnt["gy1"])

    for n in reversed(psum_order):
        open_psums.pop(n).__exit__(None, None, None)
    arena_cm.__exit__(None, None, None)
    return nc



def _prep(inputs):
    f32 = np.float32
    x = np.asarray(inputs["L0_attn_hidden_states"], f32)[0]
    xT = np.ascontiguousarray(x.T)
    xTp = np.zeros((D, L + HALO), f32)
    xTp[:, HALO:] = xT

    ipw = np.asarray(inputs["L0_attn_in_proj_w"], f32)
    w_ip = np.ascontiguousarray(
        ipw.T.reshape(KD, 128, 24, 128).transpose(2, 1, 0, 3))

    xpw = np.asarray(inputs["L0_attn_x_proj_w"], f32)
    perm = []
    for h in range(H):
        perm.extend(range(129 * h + 1, 129 * h + 65))
    for h in range(H):
        perm.extend(range(129 * h + 65, 129 * h + 129))
    dt_rows = [129 * h for h in range(H)]
    xpw_bc = xpw[perm]
    w_xp = np.ascontiguousarray(
        xpw_bc.T.reshape(KI, 128, 24, 128).transpose(2, 1, 0, 3))
    w_xpd = np.ascontiguousarray(
        xpw[dt_rows].T.reshape(KI, 128, H).transpose(1, 0, 2))

    opw = np.asarray(inputs["L0_attn_out_proj_w"], f32)
    w_op = np.ascontiguousarray(
        opw.T.reshape(KI, 128, 16, 128).transpose(2, 1, 0, 3))

    def colblock(v, nb):
        out = np.zeros((128, nb), f32)
        for o in range(nb):
            out[:, o] = v[o * 128:(o + 1) * 128]
        return out

    ipb = colblock(np.asarray(inputs["L0_attn_in_proj_b"], f32), 24)
    xpb_v = np.asarray(inputs["L0_attn_x_proj_b"], f32)
    xpb = np.zeros((128, 25), f32)
    xpb_bc = xpb_v[perm]
    for o in range(24):
        xpb[:, o] = xpb_bc[o * 128:(o + 1) * 128]
    xpb[0:24, 24] = xpb_v[dt_rows]
    opb = colblock(np.asarray(inputs["L0_attn_out_proj_b"], f32), 16)

    cw = np.asarray(inputs["L0_attn_conv_w"], f32)[:, 0, :]
    convw = np.zeros((128, 48), f32)
    for j in range(KI):
        convw[:, 4 * j:4 * (j + 1)] = cw[j * 128:(j + 1) * 128]
    convb = colblock(np.asarray(inputs["L0_attn_conv_b"], f32), 12)
    nw = np.zeros((128, KD), f32)
    nwv = np.asarray(inputs["L0_attn_norm_w"], f32)
    for k in range(KD):
        nw[:, k] = nwv[k * 128:(k + 1) * 128]

    c2 = np.zeros((H, 8), f32)
    c2[:, 0] = np.asarray(inputs["L0_attn_A_log"], f32)
    c2[:, 1] = np.asarray(inputs["L0_attn_log2_alpha_base"], f32)
    c2[:, 2] = np.asarray(inputs["L0_attn_log2_beta"], f32)
    c2[:, 3] = np.asarray(inputs["L0_attn_surprise_ema"], f32)
    rg = np.asarray(inputs["L0_attn_res_gate"], f32).reshape(1, 1)

    idf = np.eye(128, dtype=f32)
    maskp = np.where(np.arange(128)[:, None] <= np.arange(128)[None, :], 0.0, -30000.0).astype(f32)
    selh = np.zeros((H, H * 128), f32)
    for h in range(H):
        selh[h, h * 128:(h + 1) * 128] = 1.0
    sel64 = np.zeros((H, H * 64), f32)
    for h in range(H):
        sel64[h, h * 64:(h + 1) * 64] = 1.0
    selpr = np.zeros((H, 12 * 128), f32)
    for j in range(KI):
        selpr[2 * j, j * 128:j * 128 + 64] = 1.0
        selpr[2 * j + 1, j * 128 + 64:(j + 1) * 128] = 1.0
    selxb = np.zeros((128, 12 * H), f32)
    for j in range(KI):
        selxb[0:64, j * H + 2 * j] = 1.0
        selxb[64:128, j * H + 2 * j + 1] = 1.0
    ones1 = np.ones((1, 128), f32)

    wgu = np.asarray(inputs["L0_ffn_w_gate_up"], f32)
    wpr = np.asarray(inputs["L0_ffn_w_proj"], f32)

    shared = dict(w_ip=w_ip, w_xp=w_xp, w_xpd=w_xpd, w_op=w_op,
                  w_gu_nat=np.ascontiguousarray(wgu), w_pr_nat=np.ascontiguousarray(wpr),
                  ipb=ipb, xpb=xpb, opb=opb, convb=convb, convw=convw, nw=nw,
                  consts2=c2, rg=rg, idf=idf, maskp=maskp, selh=selh, sel64=sel64,
                  selxb=selxb, selpr=selpr, ones1=ones1)

    in_maps = []
    for core in range(NCORES):
        m = dict(shared)
        m["x_in"] = np.ascontiguousarray(xTp[:, TO * core:TO * core + TH])
        hm = np.ones((1, TH), f32)
        if core == 0:
            hm[0, :HALO] = 0.0
        m["hmask1"] = hm
        in_maps.append(m)
    return in_maps


def kernel(**inputs):
    in_maps = _prep(inputs)
    nc = build_program()
    res = run_bass_kernel_spmd(nc, in_maps, core_ids=list(range(NCORES)))
    out = np.concatenate([res.results[c]["y_out"] for c in range(NCORES)], axis=1)
    return np.ascontiguousarray(out.T)[None]
